# version 30
# baseline (speedup 1.0000x reference)
import math
from collections import defaultdict
from contextlib import ExitStack

import numpy as np
import ml_dtypes
import concourse.bass as bass
import concourse.mybir as mybir
from concourse.bass_utils import run_bass_kernel_spmd

F32 = mybir.dt.float32
BF16 = mybir.dt.bfloat16
I32 = mybir.dt.int32
AX = mybir.AxisListType.X
ALU = mybir.AluOpType
AF = mybir.ActivationFunctionType

B, S, D, H, DH = 16, 2048, 2048, 16, 128
NC = 8
BL = B // NC
T = BL * S
I = 8192
DC = D // 128
ICN = I // 128
TT = T // 128
TB = T // 512
EPS = 1e-5
ISQ = 1.0 / math.sqrt(DH)
NEG = -1.0e30


def _alibi_slopes(n):
    def pow2(m):
        start = 2 ** (-(2 ** (-(math.log2(m) - 3))))
        return [start * start ** i for i in range(m)]
    if math.log2(n).is_integer():
        return pow2(n)
    c = 2 ** math.floor(math.log2(n))
    return pow2(c) + _alibi_slopes(2 * c)[0::2][: n - c]


class Em:
    def __init__(self):
        self.streams = defaultdict(list)
        self.C = defaultdict(int)
        self.waited = defaultdict(dict)
        self.sems = {}

    def wait(self, eng, sem, val):
        if val <= 0:
            return
        if self.waited[eng].get(sem, -1) >= val:
            return
        self.waited[eng][sem] = val
        self.streams[eng].append(
            lambda e, s=sem, v=val: e.wait_ge(self.sems[s], v))

    def op(self, eng, sem, waits, f, n=1):
        for s, v in waits:
            self.wait(eng, s, v)
        self.C[sem] += n
        cnt = self.C[sem]
        self.streams[eng].append(
            lambda e, f=f, s=sem, n=n: f(e).then_inc(self.sems[s], n))
        return cnt

    def dma(self, eng, sem, waits, out_ap, in_ap):
        for s, v in waits:
            self.wait(eng, s, v)
        self.C[sem] += 16
        cnt = self.C[sem]
        self.streams[eng].append(
            lambda e, o=out_ap, i=in_ap, s=sem: e.dma_start(out=o, in_=i)
            .then_inc(self.sems[s], 16))
        return cnt

    def raw(self, eng, f):
        self.streams[eng].append(f)

    def replay(self, block):
        order = [("sync", block.sync), ("tensor", block.tensor),
                 ("vector", block.vector), ("scalar", block.scalar),
                 ("gpsimd", block.gpsimd)]
        for name, starter in order:
            if not self.streams[name]:
                continue
            fns = self.streams[name]

            def body(e, fns=fns):
                for fn in fns:
                    fn(e)
            starter(body)


def build_program(dbg=(), upto=3):
    nc = bass.Bass("TRN2", target_bir_lowering=False)

    def dram(name, shape, dt, kind):
        k = "ExternalOutput" if name in dbg and kind == "Internal" else kind
        return nc.dram_tensor(name, shape, dt, kind=k).ap()

    x = dram("x", [T, D], F32, "ExternalInput")
    wqT = dram("wqT", [D, D], F32, "ExternalInput")
    wkT = dram("wkT", [D, D], F32, "ExternalInput")
    wvT = dram("wvT", [D, D], F32, "ExternalInput")
    woT = dram("woT", [D, D], F32, "ExternalInput")
    wgT = dram("wgT", [D, I], I32, "ExternalInput")
    wuT = dram("wuT", [D, I], I32, "ExternalInput")
    wdT = dram("wdT", [I, D], I32, "ExternalInput")
    gsc = dram("gsc", [128, ICN], F32, "ExternalInput")
    usc = dram("usc", [128, ICN], F32, "ExternalInput")
    dsc = dram("dsc", [128, DC], F32, "ExternalInput")
    idbf = dram("idbf", [128, 128], BF16, "ExternalInput")
    idf32 = dram("idf32", [128, 128], F32, "ExternalInput")
    jrowi = dram("jrowi", [128, S], F32, "ExternalInput")
    biasbi = dram("biasbi", [H, 128, S], F32, "ExternalInput")
    dscb = dram("dscb", [128, D], F32, "ExternalInput")
    cvec = dram("cvec", [128, 8], F32, "ExternalInput")
    cmaski = dram("cmaski", [128, 128], F32, "ExternalInput")

    out = dram("out", [T, D], F32, "ExternalOutput")

    qs = dram("qs", [H, DH, T], BF16, "Internal")
    ks = dram("ks", [H, DH, T], BF16, "Internal")
    vs = dram("vs", [H, DH, T], BF16, "Internal")
    x1s = dram("x1s", [T, D], F32, "Internal")
    xn2t = dram("xn2t", [TT, 128, D], BF16, "Internal")
    wgbf = dram("wgbf", [ICN, 128, DC, 128], BF16, "Internal")
    wubf = dram("wubf", [ICN, 128, DC, 128], BF16, "Internal")
    wdbf = dram("wdbf", [ICN, 128, D], BF16, "Internal")
    wobf = dram("wobf", [D, D], BF16, "Internal")

    em = Em()

    with ExitStack() as ctx:
        def sb(name, shape, dt):
            return ctx.enter_context(nc.sbuf_tensor(name, shape, dt))

        def psum(name, shape, dt):
            return ctx.enter_context(nc.psum_tensor(name, shape, dt))

        def sem(name):
            em.sems[name] = ctx.enter_context(nc.semaphore(name))
            return name

        for s_ in ["sv", "sa", "sp", "sg",
                   "c_ld",
                   "sx0", "sx1",
                   "swt0", "swt1", "swt2",
                   "sqst0", "sqst1",
                   "sfw0", "sfw1", "sfw2", "sfw3", "sfw4", "sfw5",
                   "sfs0", "sfs1", "sfs2", "sfs3", "sfs4", "sfs5",
                   "skv0", "skv1",
                   "sbs0", "sbs1",
                   "swo0", "swo1",
                   "sxr0", "sxr1",
                   "sx1st0", "sx1st1",
                   "sx1r0", "sx1r1",
                   "sxnst0", "sxnst1",
                   "sxn0", "sxn1",
                   "sgu0", "sgu1", "sgu2",
                   "swd0", "swd1", "swd2", "swd3",
                   "sost",
                   ]:
            sem(s_)

        id_bf = sb("id_bf", [128, 128], BF16)
        cstg = [sb(f"cstg{i}", [128, 512], I32) for i in range(6)]
        cbf = [sb(f"cbf{i}", [128, 512], BF16) for i in range(6)]
        id_f32 = sb("id_f32", [128, 128], F32)
        gsc_sb = sb("gsc_sb", [128, ICN], F32)
        usc_sb = sb("usc_sb", [128, ICN], F32)
        dsc_sb = sb("dsc_sb", [128, DC], F32)
        cv = sb("cv", [128, 8], F32)

        for dst, src in [(id_bf, idbf), (id_f32, idf32), (gsc_sb, gsc),
                         (usc_sb, usc), (dsc_sb, dsc), (cv, cvec)]:
            em.dma("sync", "c_ld", [], dst[:], src)
        CLD = em.C["c_ld"]

        cast_jobs = []
        for rb in range(DC):
            for cc in range(4):
                cast_jobs.append(("f32", woT, wobf, rb, cc))
        WO_TILES = len(cast_jobs)
        for csrc, cdst, nrb, ncc, kind in (
                (wgT, wgbf, DC, I // 512, "gu"),
                (wuT, wubf, DC, I // 512, "gu"),
                (wdT, wdbf, ICN, D // 512, "wd")):
            for rb in range(nrb):
                for cc in range(ncc):
                    cast_jobs.append((kind, csrc, cdst, rb, cc))
        cast_state = {"ji": 0, "ld": 0, "done": [0] * 6,
                      "ldc": [0] * 6, "stc": [0] * 6, "wo": [0] * 6}

        def emit_casts(k):
            st_ = cast_state

            def issue_load():
                ji = st_["ld"]
                if ji >= len(cast_jobs):
                    return
                kind, csrc, cdst, rb, cc = cast_jobs[ji]
                sl = ji % 6
                stg = (cstg[sl][:] if kind in ("gu", "wd")
                       else cstg[sl][:].bitcast(F32))
                em.dma("scalar", f"sfw{sl}", [("sa", st_["done"][sl])],
                       stg, csrc[rb * 128:(rb + 1) * 128,
                                 cc * 512:(cc + 1) * 512])
                st_["ldc"][sl] = em.C[f"sfw{sl}"]
                st_["ld"] = ji + 1

            while st_["ld"] < min(st_["ji"] + 6, len(cast_jobs)):
                issue_load()
            for _ in range(k):
                ji = st_["ji"]
                if ji >= len(cast_jobs):
                    return
                kind, csrc, cdst, rb, cc = cast_jobs[ji]
                sl = ji % 6
                if kind in ("gu", "wd"):
                    st_["done"][sl] = em.op(
                        "scalar", "sa",
                        [(f"sfw{sl}", st_["ldc"][sl]),
                         (f"sfs{sl}", st_["stc"][sl])],
                        lambda e, sl=sl: e.activation(
                            out=cbf[sl][:], in_=cstg[sl][:],
                            func=AF.Copy, bias=-128.0, scale=1.0))
                else:
                    st_["done"][sl] = em.op(
                        "scalar", "sa",
                        [(f"sfw{sl}", st_["ldc"][sl]),
                         (f"sfs{sl}", st_["stc"][sl])],
                        lambda e, sl=sl: e.activation(
                            out=cbf[sl][:], in_=cstg[sl][:].bitcast(F32),
                            func=AF.Copy))
                if kind == "gu":
                    dstap = (cdst[4 * cc:4 * cc + 4, :, rb, :]
                             .rearrange("j p i -> p j i"))
                    srcap = cbf[sl][:].rearrange("p (j i) -> p j i", i=128)
                elif kind == "wd":
                    dstap = cdst[rb][:, cc * 512:(cc + 1) * 512]
                    srcap = cbf[sl][:]
                else:
                    dstap = cdst[rb * 128:(rb + 1) * 128,
                                 cc * 512:(cc + 1) * 512]
                    srcap = cbf[sl][:]
                st_["stc"][sl] = em.dma(
                    "scalar", f"sfs{sl}", [("sa", st_["done"][sl])],
                    dstap, srcap)
                if ji == WO_TILES - 1:
                    st_["wo"] = [em.C[f"sfs{i}"] for i in range(6)]
                st_["ji"] = ji + 1
                if st_["ld"] < len(cast_jobs):
                    issue_load()


        with ExitStack() as p1:
            def sb1(name, shape, dt):
                return p1.enter_context(nc.sbuf_tensor(name, shape, dt))
            xts = [sb1(f"xts{i}", [128, D], F32) for i in range(2)]
            xnbf = [sb1(f"xnbf{i}", [128, D], BF16) for i in range(2)]
            stat = sb1("stat", [128, 16], F32)
            xnfm = sb1("xnfm", [128, DC * T], BF16)
            wstage = [sb1(f"wstage{i}", [128, D], F32) for i in range(2)]
            wbf = [sb1(f"wbf{i}", [128, D], BF16) for i in range(2)]
            qstage = [sb1(f"qstage{i}", [128, 512], BF16) for i in range(2)]
            ln_ps = [p1.enter_context(nc.psum_tensor(f"lnps{i}", [128, 1024], BF16))
                     for i in range(2)]
            qkv_ps = [p1.enter_context(nc.psum_tensor(f"qkvps{i}", [128, 512], F32))
                      for i in range(6)]

            def st(slot, j):
                return stat[:, slot * 8 + j: slot * 8 + j + 1]

            norm_cnt = [0, 0]
            evac_cnt = [0, 0]
            ln_tile_done = [0] * TT
            ld_n = [0, 0]
            ln_tp = [0, 0]
            st1_saved = {}
            st2_saved = {}
            sq_saved = {}

            def ln_load(tt):
                sl = tt % 2
                ld_n[sl] = em.dma("sync", f"sx{sl}", [("sv", norm_cnt[sl])],
                                  xts[sl][:], x[tt * 128:(tt + 1) * 128, :])

            def ln_stage1(tt):
                sl = tt % 2
                c1 = em.op("vector", "sv", [(f"sx{sl}", ld_n[sl])],
                           lambda e, sl=sl: e.tensor_reduce(
                               out=st(sl, 0), in_=xts[sl][:], axis=AX,
                               op=ALU.add))
                em.op("vector", "sv", [("sv", c1)],
                      lambda e, sl=sl: e.tensor_scalar(
                          out=st(sl, 0), in0=st(sl, 0), scalar1=1.0 / D,
                          scalar2=None, op0=ALU.mult))
                sq = em.op("scalar", "sa", [(f"sx{sl}", ld_n[sl]),
                                            ("sp", ln_tp[sl])],
                           lambda e, sl=sl: e.activation(
                               out=xnbf[sl][:], in_=xts[sl][:],
                               func=AF.Square, accum_out=st(sl, 1)))
                sq_saved[tt] = sq
                c3 = em.op("vector", "sv", [("sv", c1 + 1)],
                           lambda e, sl=sl: e.tensor_scalar(
                               out=st(sl, 2), in0=st(sl, 0),
                               scalar1=st(sl, 0), scalar2=None, op0=ALU.mult))
                st1_saved[tt] = em.op(
                    "vector", "sv", [("sv", c3), ("sa", sq)],
                    lambda e, sl=sl: e.scalar_tensor_tensor(
                        out=st(sl, 3), in0=st(sl, 1), scalar=1.0 / D,
                        in1=st(sl, 2), op0=ALU.mult, op1=ALU.subtract))

            def ln_stage2(tt):
                sl = tt % 2
                em.wait("scalar", "c_ld", CLD)
                l1 = em.op("scalar", "sa", [("sv", st1_saved[tt])],
                           lambda e, sl=sl: e.activation(
                               out=st(sl, 4), in_=st(sl, 3), func=AF.Ln,
                               bias=cv[:, 0:1], scale=1.0))
                st2_saved[tt] = em.op(
                    "scalar", "sa", [("sa", l1)],
                    lambda e, sl=sl: e.activation(
                        out=st(sl, 5), in_=st(sl, 4), func=AF.Exp,
                        bias=0.0, scale=-0.5))

            def ln_stage3(tt):
                sl = tt % 2
                norm_cnt[sl] = em.op(
                    "vector", "sv", [("sa", st2_saved[tt])],
                    lambda e, sl=sl: e.tensor_scalar(
                        out=xnbf[sl][:], in0=xts[sl][:], scalar1=st(sl, 0),
                        scalar2=st(sl, 5), op0=ALU.subtract, op1=ALU.mult))
                em.wait("tensor", "c_ld", CLD)
                em.wait("tensor", "sv", norm_cnt[sl])
                for half in range(2):
                    em.wait("tensor", "sv", evac_cnt[half])
                    tp = 0
                    for k in range(8):
                        dc = half * 8 + k
                        tp = em.op("tensor", "sp", [],
                                   lambda e, sl=sl, dc=dc, k=k, half=half:
                                   e.transpose(
                                       out=ln_ps[half][:, k * 128:(k + 1) * 128],
                                       in_=xnbf[sl][:, dc * 128:(dc + 1) * 128],
                                       identity=id_bf[:]))
                    evac_cnt[half] = em.op(
                        "vector", "sv", [("sp", tp)],
                        lambda e, half=half, tt=tt: e.tensor_copy(
                            out=xnfm[:].rearrange("p (dc t) -> p dc t", dc=DC)
                            [:, half * 8:half * 8 + 8, tt * 128:(tt + 1) * 128],
                            in_=ln_ps[half][:].rearrange(
                                "p (dc t) -> p dc t", t=128)))
                    ln_tp[sl] = tp
                ln_tile_done[tt] = evac_cnt[1]

            ln_load(0)
            for tt in range(TT + 1):
                emit_casts(3)
                if tt >= 1:
                    ln_stage3(tt - 1)
                if tt < TT:
                    if tt + 1 < TT:
                        ln_load(tt + 1)
                    ln_stage1(tt)
                    ln_stage2(tt)
            LN_DONE = em.C["sv"]

            wT_srcs = [(wqT, qs), (wkT, ks), (wvT, vs)]
            cast_cnt = [0, 0]
            mm_last_for_wbf = [0, 0]
            mm_last_for_wstage = [0, 0]
            evq_cnt = [0] * 6
            evq_eng = [None] * 6
            qst_cnt = [0, 0]
            n_ev = 0
            NW = 3 * DC
            wload_cnt = [0, 0]

            def w_load(gi):
                pi_, oc_ = gi // DC, gi % DC
                sl = gi % 2
                wload_cnt[sl] = em.dma(
                    "sync", f"swt{sl}", [("sv", cast_cnt[sl])],
                    wstage[sl][:].rearrange("p (dc o) -> p dc o", o=128),
                    wT_srcs[pi_][0][:, oc_ * 128:(oc_ + 1) * 128]
                    .rearrange("(dc p) o -> p dc o", p=128))

            def w_cast(gi):
                sl = gi % 2
                cast_cnt[sl] = em.op(
                    "vector", "sv",
                    [(f"swt{sl}", wload_cnt[sl]), ("sp", mm_last_for_wbf[sl])],
                    lambda e, sl=sl: e.tensor_copy(
                        out=wbf[sl][:], in_=wstage[sl][:]))

            w_load(0)
            w_cast(0)
            w_load(1)
            for gi in range(NW):
                pi, oc = gi // DC, gi % DC
                dst = wT_srcs[pi][1]
                sl = gi % 2
                if gi + 1 < NW:
                    w_cast(gi + 1)
                if gi + 2 < NW:
                    w_load(gi + 2)
                em.wait("tensor", "sv", cast_cnt[sl])
                for tbg, tbs in enumerate(((0, 1, 2), (3, 4, 5), (6, 7))):
                    bset = (tbg % 2) * 3
                    for j in range(len(tbs)):
                        bk = bset + j
                        if evq_eng[bk] is not None:
                            em.wait("tensor", evq_eng[bk], evq_cnt[bk])
                    em.wait("tensor", "sv",
                            ln_tile_done[tbs[-1] * 4 + 3])
                    mmn = 0
                    for dc in range(DC):
                        for j, tb in enumerate(tbs):
                            bk = bset + j
                            mmn = em.op(
                                "tensor", "sp", [],
                                lambda e, sl=sl, dc=dc, bk=bk, tb=tb:
                                e.matmul(
                                    out=qkv_ps[bk][:],
                                    lhsT=wbf[sl][:, dc * 128:(dc + 1) * 128],
                                    rhs=xnfm[:, dc * T + tb * 512:
                                             dc * T + tb * 512 + 512],
                                    start=(dc == 0), stop=(dc == DC - 1)))
                    mm_last_for_wbf[sl] = mmn
                    for j, tb in enumerate(tbs):
                        bk = bset + j
                        ssl = n_ev % 2
                        ssem = f"sqst{ssl}"
                        if pi == 0:
                            ev = em.op("vector", "sv",
                                       [("sp", mmn), (ssem, qst_cnt[ssl])],
                                       lambda e, bk=bk, ssl=ssl:
                                       e.tensor_scalar(
                                           out=qstage[ssl][:],
                                           in0=qkv_ps[bk][:],
                                           scalar1=ISQ, scalar2=None,
                                           op0=ALU.mult))
                        else:
                            ev = em.op("vector", "sv",
                                       [("sp", mmn), (ssem, qst_cnt[ssl])],
                                       lambda e, bk=bk, ssl=ssl:
                                       e.tensor_copy(
                                           out=qstage[ssl][:],
                                           in_=qkv_ps[bk][:]))
                        evq_cnt[bk], evq_eng[bk] = ev, "sv"
                        qst_cnt[ssl] = em.dma(
                            "sync", ssem, [("sv", ev)],
                            dst[oc][:, tb * 512: tb * 512 + 512],
                            qstage[ssl][:])
                        n_ev += 1
                    emit_casts(6)
            QKV_ST = (em.C["sqst0"], em.C["sqst1"])

        def barrier(engines=("sync", "tensor", "vector", "scalar"),
                    dma_sems=()):
            targets = [(s_, em.C[s_]) for s_ in ("sv", "sa", "sp")]
            targets += [(s_, em.C[s_]) for s_ in dma_sems]
            for eng in engines:
                for s_, v_ in targets:
                    em.wait(eng, s_, v_)

        ALLE = ("sync", "tensor", "vector", "scalar", "gpsimd")
        barrier(engines=ALLE, dma_sems=("sqst0", "sqst1"))
        em.wait("gpsimd", "sg", em.C["sg"])

        if upto < 2:
            em.wait("sync", "sqst0", QKV_ST[0])
            em.wait("sync", "sqst1", QKV_ST[1])
            with nc.Block() as block:
                em.replay(block)
            return nc

        with ExitStack() as p2:
            def sb2(name, shape, dt):
                return p2.enter_context(nc.sbuf_tensor(name, shape, dt))
            kh = [sb2(f"kh{i}", [128, S], BF16) for i in range(2)]
            qh = [sb2(f"qh{i}", [128, S], BF16) for i in range(2)]
            vh = [sb2(f"vh{i}", [128, S], BF16) for i in range(2)]
            vt = [sb2(f"vt{i}", [128, S], BF16) for i in range(2)]
            sf = [sb2(f"sf{i}", [128, S], F32) for i in range(3)]
            pb = [sb2(f"pb{i}", [128, S], BF16) for i in range(3)]
            pt = [sb2(f"pt{i}", [128, S], BF16) for i in range(3)]
            biash = [sb2(f"biash{i}", [128, S], F32) for i in range(2)]
            cmask = sb2("cmask", [128, 128], F32)
            Zb = sb2("Zb", [128, H * S], BF16)
            st2 = sb2("st2", [128, 32], F32)
            wo_sb = [sb2(f"wo_sb{i}", [128, DC * 256], BF16) for i in range(2)]
            xres = [sb2(f"xres{i}", [128, 256], F32) for i in range(2)]
            x1st = [sb2(f"x1st{i}", [128, 256], F32) for i in range(2)]
            sc_ps = [p2.enter_context(nc.psum_tensor(f"scps{i}", [128, 512], F32))
                     for i in range(4)]
            tp_ps = [p2.enter_context(nc.psum_tensor(f"tpps{i}", [128, 1024], BF16))
                     for i in range(2)]
            z_ps = [p2.enter_context(nc.psum_tensor(f"zps{i}", [128, 128], F32))
                    for i in range(2)]
            o_ps = sc_ps

            em.dma("sync", "c_ld", [], cmask[:], cmaski)
            CLD2 = em.C["c_ld"]

            slot_war = [0, 0]
            tp_war = [("sa", 0), ("sa", 0)]
            p1_war = [0, 0, 0, 0]
            kv_ld = [0, 0]
            slopes = _alibi_slopes(H)
            ccg = 0

            def stv(j):
                return st2[:, j:j + 1]

            def emit_head_loads(s, h):
                sl = (s * H + h) % 2
                sem_ = f"skv{sl}"
                for buf, src_ in ((kh, ks), (qh, qs), (vh, vs)):
                    em.dma("sync", sem_, [("sp", em.C["sp"])],
                           buf[sl][:], src_[h][:, s * S:(s + 1) * S])
                kv_ld[sl] = em.C[sem_]

            for s in range(BL):
                if s == 0:
                    emit_head_loads(s, 0)
                    emit_head_loads(s, 1)
                NCH = H * 16
                p1l = {}
                negl = {}
                expl = {}
                recl = {}
                pscl = {}
                egrp = {}
                flast = {}
                gmm = {}
                hl = {}
                vtr = {}
                bias_war = [0, 0]

                def A_(i):
                    nonlocal ccg
                    h, qt = i // 16, i % 16
                    sl = (s * H + h) % 2
                    qp = i % 3
                    kend = (qt + 1) * 128
                    qsl = qh[sl][:, qt * 128:(qt + 1) * 128]
                    for c in range((kend + 511) // 512):
                        N = min(512, kend - c * 512)
                        pp = ccg % 4
                        mmn = em.op(
                            "tensor", "sp",
                            [("sa", p1_war[pp]), (f"skv{sl}", kv_ld[sl])],
                            lambda e, pp=pp, c=c, N=N, sl=sl, qsl=qsl:
                            e.matmul(out=sc_ps[pp][:, :N], lhsT=qsl,
                                     rhs=kh[sl][:, c * 512: c * 512 + N],
                                     start=True, stop=True))
                        p1_war[pp] = em.op(
                            "scalar", "sa",
                            [("sp", mmn), ("sa", expl.get(i - 3, 0))],
                            lambda e, pp=pp, c=c, N=N, qp=qp: e.activation(
                                out=sf[qp][:, c * 512: c * 512 + N],
                                in_=sc_ps[pp][:, :N], func=AF.Copy))
                        p1l[i] = p1_war[pp]
                        ccg += 1

                def TT_(i):
                    h, qt = i // 16, i % 16
                    qp = i % 3
                    kend = (qt + 1) * 128
                    em.wait("vector", "c_ld", CLD2)
                    c1 = em.op("vector", "sv", [("sa", p1l[i])],
                               lambda e, qp=qp, kend=kend: e.tensor_scalar(
                                   out=sf[qp][:, :kend], in0=sf[qp][:, :kend],
                                   scalar1=1024.0, scalar2=-1024.0,
                                   op0=ALU.min, op1=ALU.max))
                    c2 = em.op("vector", "sv", [("sv", c1)],
                               lambda e, qp=qp, qt=qt: e.tensor_tensor(
                                   out=sf[qp][:, qt * 128:(qt + 1) * 128],
                                   in0=sf[qp][:, qt * 128:(qt + 1) * 128],
                                   in1=cmask[:], op=ALU.add))
                    c3 = em.op("vector", "sv",
                               [("sv", c2), (f"sbs{h % 2}", bias_war[h % 2])],
                               lambda e, qp=qp, kend=kend, h=h: e.tensor_tensor(
                                   out=sf[qp][:, :kend], in0=sf[qp][:, :kend],
                                   in1=biash[h % 2][:, :kend], op=ALU.add))
                    negl[i] = em.op(
                        "vector", "sv", [("sv", c3)],
                        lambda e, qp=qp, kend=kend: e.tensor_reduce(
                            out=stv(qp * 5 + 3), in_=sf[qp][:, :kend],
                            axis=AX, op=ALU.max, negate=True))

                def C_(i):
                    h, qt = i // 16, i % 16
                    qp = i % 3
                    kend = (qt + 1) * 128
                    expl[i] = em.op(
                        "scalar", "sa",
                        [("sv", negl[i]),
                         ("sp", egrp.get(i - 3, [(0, 0, 0, 0)])[-1][3]),
                         ("sv", recl.get(i - 3, 0))],
                        lambda e, qp=qp, kend=kend: e.activation(
                            out=pb[qp][:, :kend], in_=sf[qp][:, :kend],
                            func=AF.Exp, bias=stv(qp * 5 + 3), scale=1.0,
                            accum_out=stv(qp * 5 + 1)))

                def D_(i):
                    qp = i % 3
                    recl[i] = em.op("vector", "sv", [("sa", expl[i])],
                                    lambda e, qp=qp: e.reciprocal(
                                        out=stv(qp * 5 + 2),
                                        in_=stv(qp * 5 + 1)))

                def PS_(i):
                    qp = i % 3
                    kend = (i % 16 + 1) * 128
                    pscl[i] = em.op(
                        "vector", "sv", [("sv", recl[i])],
                        lambda e, qp=qp, kend=kend: e.tensor_scalar(
                            out=pb[qp][:, :kend], in0=pb[qp][:, :kend],
                            scalar1=stv(qp * 5 + 2), scalar2=None,
                            op0=ALU.mult))

                def E_(i):
                    qp = i % 3
                    kch = i % 16 + 1
                    groups = []
                    em.wait("tensor", "sv", pscl[i])
                    for g in range((kch + 7) // 8):
                        gb = g % 2
                        ntl = min(8, kch - g * 8)
                        em.wait("tensor", tp_war[gb][0], tp_war[gb][1])
                        tpn = 0
                        for k in range(ntl):
                            kc = g * 8 + k
                            tpn = em.op(
                                "tensor", "sp", [],
                                lambda e, qp=qp, kc=kc, gb=gb, k=k: e.transpose(
                                    out=tp_ps[gb][:, k * 128:(k + 1) * 128],
                                    in_=pb[qp][:, kc * 128:(kc + 1) * 128],
                                    identity=id_bf[:]))
                        groups.append((g, gb, ntl, tpn))
                    egrp[i] = groups

                def F_(i):
                    qp = i % 3
                    last = 0
                    for g, gb, ntl, tpn in egrp[i]:
                        last = em.op(
                            "scalar", "sa",
                            [("sp", tpn), ("sp", gmm.get(i - 3, 0))],
                            lambda e, qp=qp, g=g, gb=gb, ntl=ntl: e.activation(
                                out=pt[qp][:, g * 1024: g * 1024 + ntl * 128],
                                in_=tp_ps[gb][:, :ntl * 128], func=AF.Copy))
                        tp_war[gb] = ("sa", last)
                    flast[i] = last

                def G_(i):
                    h, qt = i // 16, i % 16
                    sl = (s * H + h) % 2
                    qp = i % 3
                    zp = i % 2
                    kch = qt + 1
                    em.wait("tensor", "sa", flast[i])
                    em.wait("tensor", "sv", vtr[h])
                    em.wait("tensor", "sv", hl.get(i - 2, 0))
                    mmn = 0
                    for kc in range(kch):
                        mmn = em.op(
                            "tensor", "sp", [],
                            lambda e, qp=qp, kc=kc, sl=sl, zp=zp, kch=kch:
                            e.matmul(
                                out=z_ps[zp][:],
                                lhsT=vt[sl][:, kc * 128:(kc + 1) * 128],
                                rhs=pt[qp][:, kc * 128:(kc + 1) * 128],
                                start=(kc == 0), stop=(kc == kch - 1)))
                    gmm[i] = mmn

                def H_(i):
                    h, qt = i // 16, i % 16
                    zp = i % 2
                    hl[i] = em.op(
                        "vector", "sv", [("sp", gmm[i])],
                        lambda e, zp=zp, h=h, qt=qt: e.tensor_copy(
                            out=Zb[:, h * S + qt * 128: h * S + (qt + 1) * 128],
                            in_=z_ps[zp][:]))

                def headstart(h):
                    sl = (s * H + h) % 2
                    emit_casts(1)
                    if h == 0:
                        bias_war[0] = em.dma(
                            "sync", "sbs0", [("sv", em.C["sv"])],
                            biash[0][:], biasbi[0])
                    em.wait("tensor", f"skv{sl}", kv_ld[sl])
                    vt_ready = 0
                    for g in range(2):
                        em.wait("tensor", tp_war[g][0], tp_war[g][1])
                        tpn = 0
                        for k in range(8):
                            kc = g * 8 + k
                            tpn = em.op(
                                "tensor", "sp", [],
                                lambda e, sl=sl, kc=kc, g=g, k=k: e.transpose(
                                    out=tp_ps[g][:, k * 128:(k + 1) * 128],
                                    in_=vh[sl][:, kc * 128:(kc + 1) * 128],
                                    identity=id_bf[:]))
                        vt_ready = em.op(
                            "vector", "sv",
                            [("sp", tpn), ("sp", slot_war[sl])],
                            lambda e, sl=sl, g=g: e.tensor_copy(
                                out=vt[sl][:, g * 1024:(g + 1) * 1024],
                                in_=tp_ps[g][:]))
                        tp_war[g] = ("sv", vt_ready)
                    vtr[h] = vt_ready
                    nh = s * H + h + 1
                    if nh < BL * H and nh >= 2:
                        emit_head_loads(nh // H, nh % H)

                for i in range(NCH + 5):
                    if i < NCH:
                        h, qt = i // 16, i % 16
                        if qt == 0:
                            headstart(h)
                        if qt == 1 and h + 1 < H:
                            bias_war[(h + 1) % 2] = em.dma(
                                "sync", f"sbs{(h + 1) % 2}",
                                [("sv", em.C["sv"])],
                                biash[(h + 1) % 2][:], biasbi[h + 1])
                        A_(i)
                    if 0 <= i - 1 < NCH:
                        TT_(i - 1)
                    if 0 <= i - 2 < NCH:
                        C_(i - 2)
                    if 0 <= i - 3 < NCH:
                        D_(i - 3)
                        PS_(i - 3)
                    if 0 <= i - 4 < NCH:
                        E_(i - 4)
                        F_(i - 4)
                    if 0 <= i - 5 < NCH:
                        G_(i - 5)
                        H_(i - 5)
                    if (i - 5) >= 0 and (i - 5) % 16 == 15:
                        hdone = (i - 5) // 16
                        slot_war[(s * H + hdone) % 2] = em.C["sp"]
                Z_DONE = em.C["sv"]

                barrier()
                emit_casts(150)
                wo_ld = [0, 0]
                ops_war = [0, 0]
                x1_evac = [0, 0]
                bank_evac = [0, 0, 0, 0]
                xr_ld = [0, 0]
                x1_stc = [0, 0]
                for ci in range(6):
                    em.wait("sync", f"sfs{ci}", cast_state["wo"][ci])

                def wo_load(dblk):
                    wsl = dblk % 2
                    wo_ld[wsl] = em.dma(
                        "sync", f"swo{wsl}", [("sp", ops_war[wsl])],
                        wo_sb[wsl][:].rearrange("p (oc d) -> p oc d", d=256),
                        wobf[:, dblk * 256:(dblk + 1) * 256]
                        .rearrange("(oc p) d -> p oc d", p=128))

                wo_load(0)
                for dblk in range(8):
                    wsl = dblk % 2
                    if dblk + 1 < 8:
                        wo_load(dblk + 1)
                    for tc in range(16):
                        pp4 = (dblk * 16 + tc) % 4
                        pp2 = (dblk * 16 + tc) % 2
                        xr_ld[pp2] = em.dma(
                            "sync", f"sxr{pp2}", [("sv", x1_evac[pp2])],
                            xres[pp2][:],
                            x[s * S + tc * 128: s * S + (tc + 1) * 128,
                              dblk * 256:(dblk + 1) * 256])
                        em.wait("tensor", f"swo{wsl}", wo_ld[wsl])
                        em.wait("tensor", "sv", bank_evac[pp4])
                        mmn = 0
                        for oc in range(DC):
                            mmn = em.op(
                                "tensor", "sp", [],
                                lambda e, pp4=pp4, oc=oc, tc=tc, wsl=wsl:
                                e.matmul(
                                    out=o_ps[pp4][:, :256],
                                    lhsT=Zb[:, oc * S + tc * 128:
                                            oc * S + (tc + 1) * 128],
                                    rhs=wo_sb[wsl][:, oc * 256:(oc + 1) * 256],
                                    start=(oc == 0), stop=(oc == DC - 1)))
                        ops_war[wsl] = mmn
                        ev_ = em.op(
                            "vector", "sv",
                            [("sp", mmn), (f"sxr{pp2}", xr_ld[pp2]),
                             (f"sx1st{pp2}", x1_stc[pp2])],
                            lambda e, pp4=pp4, pp2=pp2: e.tensor_tensor(
                                out=x1st[pp2][:], in0=o_ps[pp4][:, :256],
                                in1=xres[pp2][:], op=ALU.add))
                        x1_evac[pp2] = ev_
                        bank_evac[pp4] = ev_
                        x1_stc[pp2] = em.dma(
                            "sync", f"sx1st{pp2}", [("sv", ev_)],
                            x1s[s * S + tc * 128: s * S + (tc + 1) * 128,
                                dblk * 256:(dblk + 1) * 256],
                            x1st[pp2][:])

                barrier(dma_sems=("sx1st0", "sx1st1"))
                xn_stc = [0, 0]
                ln2_nrm = [0, 0]
                l2_tp = [0, 0]
                l2_ld = [0, 0]
                l2_s1 = {}
                l2_s2 = {}

                def l2_load(t16):
                    sl = t16 % 2
                    l2_ld[sl] = em.dma(
                        "sync", f"sx1r{sl}", [("sv", ln2_nrm[sl])],
                        sf[sl][:], x1s[(s * 16 + t16) * 128:
                                       (s * 16 + t16 + 1) * 128, :])

                def l2_stage1(t16):
                    sl = t16 % 2
                    c1 = em.op("vector", "sv", [(f"sx1r{sl}", l2_ld[sl])],
                               lambda e, sl=sl: e.tensor_reduce(
                                   out=stv(16 + sl * 6), in_=sf[sl][:],
                                   axis=AX, op=ALU.add))
                    em.op("vector", "sv", [("sv", c1)],
                          lambda e, sl=sl: e.tensor_scalar(
                              out=stv(16 + sl * 6), in0=stv(16 + sl * 6),
                              scalar1=1.0 / D, scalar2=None, op0=ALU.mult))
                    sq = em.op("scalar", "sa",
                               [(f"sx1r{sl}", l2_ld[sl]),
                                ("sp", l2_tp[sl])],
                               lambda e, sl=sl: e.activation(
                                   out=pb[sl][:], in_=sf[sl][:],
                                   func=AF.Square,
                                   accum_out=stv(17 + sl * 6)))
                    c3 = em.op("vector", "sv", [("sv", c1 + 1)],
                               lambda e, sl=sl: e.tensor_scalar(
                                   out=stv(18 + sl * 6), in0=stv(16 + sl * 6),
                                   scalar1=stv(16 + sl * 6), scalar2=None,
                                   op0=ALU.mult))
                    l2_s1[t16] = em.op(
                        "vector", "sv", [("sv", c3), ("sa", sq)],
                        lambda e, sl=sl: e.scalar_tensor_tensor(
                            out=stv(19 + sl * 6), in0=stv(17 + sl * 6),
                            scalar=1.0 / D, in1=stv(18 + sl * 6),
                            op0=ALU.mult, op1=ALU.subtract))

                def l2_stage2(t16):
                    sl = t16 % 2
                    l1 = em.op("scalar", "sa", [("sv", l2_s1[t16])],
                               lambda e, sl=sl: e.activation(
                                   out=stv(20 + sl * 6), in_=stv(19 + sl * 6),
                                   func=AF.Ln, bias=cv[:, 0:1], scale=1.0))
                    l2_s2[t16] = em.op(
                        "scalar", "sa", [("sa", l1)],
                        lambda e, sl=sl: e.activation(
                            out=stv(21 + sl * 6), in_=stv(20 + sl * 6),
                            func=AF.Exp, bias=0.0, scale=-0.5))

                def l2_stage3(t16):
                    sl = t16 % 2
                    tt = s * 16 + t16
                    nrm = em.op("vector", "sv", [("sa", l2_s2[t16])],
                                lambda e, sl=sl: e.tensor_scalar(
                                    out=pb[sl][:], in0=sf[sl][:],
                                    scalar1=stv(16 + sl * 6),
                                    scalar2=stv(21 + sl * 6),
                                    op0=ALU.subtract, op1=ALU.mult))
                    ln2_nrm[sl] = nrm
                    em.wait("tensor", "sv", nrm)
                    evl = 0
                    for g in range(2):
                        em.wait("tensor", tp_war[g][0], tp_war[g][1])
                        tpn = 0
                        for k in range(8):
                            dc = g * 8 + k
                            tpn = em.op("tensor", "sp", [],
                                        lambda e, sl=sl, dc=dc, g=g, k=k:
                                        e.transpose(
                                            out=tp_ps[g][:, k * 128:(k + 1) * 128],
                                            in_=pb[sl][:, dc * 128:(dc + 1) * 128],
                                            identity=id_bf[:]))
                        evl = em.op("scalar", "sa",
                                    [("sp", tpn),
                                     (f"sxnst{sl}", xn_stc[sl])],
                                    lambda e, sl=sl, g=g: e.activation(
                                        out=pt[sl][:, g * 1024:(g + 1) * 1024],
                                        in_=tp_ps[g][:], func=AF.Copy))
                        tp_war[g] = ("sa", evl)
                        l2_tp[sl] = tpn
                    xn_stc[sl] = em.dma("sync", f"sxnst{sl}", [("sa", evl)],
                                        xn2t[tt], pt[sl][:])

                l2_load(0)
                for t16 in range(17):
                    if t16 >= 1:
                        l2_stage3(t16 - 1)
                    if t16 < 16:
                        if t16 + 1 < 16:
                            l2_load(t16 + 1)
                        l2_stage1(t16)
                        l2_stage2(t16)
                emit_casts(150)
                barrier(dma_sems=("sxnst0", "sxnst1"))

            emit_casts(len(cast_jobs))
            CAST_ST = tuple(em.C[f"sfs{i}"] for i in range(6))
            X1_ST = (em.C["sx1st0"], em.C["sx1st1"])
            XN_ST = (em.C["sxnst0"], em.C["sxnst1"])

        if upto < 3:
            for s_, v_ in ([("sx1st0", X1_ST[0]), ("sx1st1", X1_ST[1]),
                            ("sxnst0", XN_ST[0]), ("sxnst1", XN_ST[1]),
                            ("sqst0", QKV_ST[0]), ("sqst1", QKV_ST[1])] +
                           [(f"sfs{i}", CAST_ST[i]) for i in range(6)]):
                em.wait("sync", s_, v_)
            with nc.Block() as block:
                em.replay(block)
            return nc

        barrier(engines=ALLE,
                dma_sems=("sxnst0", "sxnst1", "sx1st0", "sx1st1", "sfs0",
                          "sfs1", "sfs2", "sfs3", "sfs4", "sfs5",
                          "sqst0", "sqst1"))
        em.wait("gpsimd", "sg", em.C["sg"])
        with ExitStack() as p3:
            def sb3(name, shape, dt):
                return p3.enter_context(nc.sbuf_tensor(name, shape, dt))
            xn2_sb = [sb3(f"xn2sb{i}", [128, DC * 512], BF16) for i in range(2)]
            wg_sb = [sb3(f"wgsb{i}", [128, DC * 128], BF16) for i in range(3)]
            wu_sb = [sb3(f"wusb{i}", [128, DC * 128], BF16) for i in range(3)]
            h_sb = sb3("hsb", [128, ICN * 512], BF16)
            wd_sb = [sb3(f"wdsb{i}", [128, 1024], BF16) for i in range(4)]
            dscb_sb = sb3("dscb_sb", [128, D], F32)
            x1t = [sb3(f"x1t{i}", [128, D], F32) for i in range(2)]
            ost = [sb3(f"ost{i}", [128, D], F32) for i in range(2)]
            ps8 = [p3.enter_context(nc.psum_tensor(f"ps8_{i}", [128, 512], F32))
                   for i in range(8)]
            em.dma("sync", "c_ld", [], dscb_sb[:], dscb)
            CLD3 = em.C["c_ld"]
            gu_ps = ps8

            xn_ld = [0, 0]
            gu_ld = [0, 0, 0]
            wd_ld = [0, 0, 0, 0]
            x1_ld = [0, 0]
            x1t_war = [0, 0]
            gu_mm_last = [0, 0, 0]
            xn_mm_last = [0, 0]
            wd_mm_last = [0, 0, 0, 0]
            bank_war = [("sv", 0)] * 8
            down_last = 0
            ostc = [0, 0]
            for tb in range(TB):
                xsl = tb % 2
                xn_ld[xsl] = em.dma(
                    "sync", f"sxn{xsl}", [("sp", xn_mm_last[xsl])],
                    xn2_sb[xsl][:].rearrange("p (dc tt c) -> p dc tt c",
                                             dc=DC, tt=4),
                    xn2t[tb * 4:(tb + 1) * 4]
                    .rearrange("tt p (dc c) -> p dc tt c", c=128))
                for ic in range(ICN):
                    sl = ic % 3
                    sgu = f"sgu{sl}"
                    em.dma("gpsimd", sgu, [("sp", gu_mm_last[sl])],
                           wg_sb[sl][:],
                           wgbf[ic].rearrange("p dc i -> p (dc i)"))
                    gu_ld[sl] = em.dma(
                        "gpsimd", sgu, [],
                        wu_sb[sl][:],
                        wubf[ic].rearrange("p dc i -> p (dc i)"))
                    em.wait("tensor", sgu, gu_ld[sl])
                    em.wait("tensor", f"sxn{xsl}", xn_ld[xsl])
                    em.wait("tensor", bank_war[2 * sl][0], bank_war[2 * sl][1])
                    em.wait("tensor", bank_war[2 * sl + 1][0],
                            bank_war[2 * sl + 1][1])
                    g_mm = u_mm = 0
                    for dc in range(DC):
                        g_mm = em.op(
                            "tensor", "sp", [],
                            lambda e, sl=sl, dc=dc, xsl=xsl: e.matmul(
                                out=gu_ps[2 * sl][:],
                                lhsT=wg_sb[sl][:, dc * 128:(dc + 1) * 128],
                                rhs=xn2_sb[xsl][:, dc * 512:(dc + 1) * 512],
                                start=(dc == 0), stop=(dc == DC - 1)))
                    for dc in range(DC):
                        u_mm = em.op(
                            "tensor", "sp", [],
                            lambda e, sl=sl, dc=dc, xsl=xsl: e.matmul(
                                out=gu_ps[2 * sl + 1][:],
                                lhsT=wu_sb[sl][:, dc * 128:(dc + 1) * 128],
                                rhs=xn2_sb[xsl][:, dc * 512:(dc + 1) * 512],
                                start=(dc == 0), stop=(dc == DC - 1)))
                    gu_mm_last[sl] = u_mm
                    xn_mm_last[xsl] = u_mm
                    sv_ = em.op("scalar", "sa",
                                [("sp", g_mm), ("sp", down_last)],
                                lambda e, sl=sl, ic=ic: e.activation(
                                    out=h_sb[:, ic * 512:(ic + 1) * 512],
                                    in_=gu_ps[2 * sl][:], func=AF.Silu,
                                    scale=gsc_sb[:, ic:ic + 1]))
                    bank_war[2 * sl] = ("sa", sv_)
                    hm = em.op(
                        "vector", "sv", [("sp", u_mm), ("sa", sv_)],
                        lambda e, sl=sl, ic=ic: e.scalar_tensor_tensor(
                            out=h_sb[:, ic * 512:(ic + 1) * 512],
                            in0=gu_ps[2 * sl + 1][:],
                            scalar=usc_sb[:, ic:ic + 1],
                            in1=h_sb[:, ic * 512:(ic + 1) * 512],
                            op0=ALU.mult, op1=ALU.mult))
                    bank_war[2 * sl + 1] = ("sv", hm)
                H_DONE = em.C["sv"]
                em.wait("vector", "c_ld", CLD3)
                for dhh in range(2):
                    for ic in range(ICN):
                        sl = ic % 4
                        swd = f"swd{sl}"
                        wd_ld[sl] = em.dma(
                            "scalar", swd, [("sp", wd_mm_last[sl])],
                            wd_sb[sl][:],
                            wdbf[ic][:, dhh * 1024:(dhh + 1) * 1024])
                        em.wait("tensor", swd, wd_ld[sl])
                        if ic == 0:
                            em.wait("tensor", "sv", H_DONE)
                        mmn = 0
                        for tc in range(4):
                            for db in range(2):
                                if ic == 0:
                                    em.wait("tensor", bank_war[tc * 2 + db][0],
                                            bank_war[tc * 2 + db][1])
                                mmn = em.op(
                                    "tensor", "sp", [],
                                    lambda e, sl=sl, ic=ic, tc=tc,
                                    db=db: e.matmul(
                                        out=ps8[tc * 2 + db][:],
                                        lhsT=h_sb[:, ic * 512 + tc * 128:
                                                  ic * 512 + (tc + 1) * 128],
                                        rhs=wd_sb[sl][:, db * 512:(db + 1) * 512],
                                        start=(ic == 0), stop=(ic == ICN - 1)))
                        wd_mm_last[sl] = mmn
                        down_last = mmn
                    for tc in range(4):
                        u = tc % 2
                        x1_ld[u] = em.dma(
                            "sync", f"sx1r{u}", [("sv", x1t_war[u])],
                            x1t[u][:, :1024],
                            x1s[(tb * 4 + tc) * 128:(tb * 4 + tc + 1) * 128,
                                dhh * 1024:(dhh + 1) * 1024])
                        ev = 0
                        for db in range(2):
                            ev = em.op(
                                "vector", "sv",
                                [("sp", down_last),
                                 ("sost", ostc[u])],
                                lambda e, db=db, tc=tc, u=u, dhh=dhh:
                                e.tensor_tensor(
                                    out=ost[u][:, db * 512:(db + 1) * 512],
                                    in0=ps8[tc * 2 + db][:],
                                    in1=dscb_sb[:, dhh * 1024 + db * 512:
                                                dhh * 1024 + (db + 1) * 512],
                                    op=ALU.mult))
                            bank_war[tc * 2 + db] = ("sv", ev)
                        rs = em.op("vector", "sv",
                                   [("sv", ev),
                                    (f"sx1r{u}", x1_ld[u])],
                                   lambda e, u=u: e.tensor_tensor(
                                       out=ost[u][:, :1024],
                                       in0=ost[u][:, :1024],
                                       in1=x1t[u][:, :1024], op=ALU.add))
                        x1t_war[u] = rs
                        ostc[u] = em.dma(
                            "sync", "sost", [("sv", rs)],
                            out[(tb * 4 + tc) * 128:(tb * 4 + tc + 1) * 128,
                                dhh * 1024:(dhh + 1) * 1024],
                            ost[u][:, :1024])

        em.wait("sync", "sost", em.C["sost"])
        for s_, v_ in ([("sqst0", QKV_ST[0]), ("sqst1", QKV_ST[1]),
                        ("sx1st0", X1_ST[0]), ("sx1st1", X1_ST[1]),
                        ("sxnst0", XN_ST[0]), ("sxnst1", XN_ST[1])] +
                       [(f"sfs{i}", CAST_ST[i]) for i in range(6)]):
            em.wait("sync", s_, v_)

        with nc.Block() as block:
            em.replay(block)

    return nc


def _prep_inputs(inputs):
    x = np.asarray(inputs["L0_attn_resid_pre"], dtype=np.float32)
    tw = lambda a: np.ascontiguousarray(np.asarray(a).T)
    base = {
        "wqT": tw(inputs["L0_attn_Wq"]).astype(np.float32, copy=False),
        "wkT": tw(inputs["L0_attn_Wk"]).astype(np.float32, copy=False),
        "wvT": tw(inputs["L0_attn_Wv"]).astype(np.float32, copy=False),
        "woT": tw(inputs["L0_attn_Wo"]).astype(np.float32, copy=False),
        "wgT": tw(inputs["L0_ffn_gate_w"]),
        "wuT": tw(inputs["L0_ffn_up_w"]),
        "wdT": tw(inputs["L0_ffn_down_w"]),
        "gsc": np.ascontiguousarray(
            np.asarray(inputs["L0_ffn_gate_scale"], dtype=np.float32)
            .reshape(ICN, 128).T),
        "usc": np.ascontiguousarray(
            np.asarray(inputs["L0_ffn_up_scale"], dtype=np.float32)
            .reshape(ICN, 128).T),
        "dsc": np.ascontiguousarray(
            np.asarray(inputs["L0_ffn_down_scale"], dtype=np.float32)
            .reshape(DC, 128).T),
        "idbf": np.eye(128, dtype=ml_dtypes.bfloat16),
        "idf32": np.eye(128, dtype=np.float32),
        "cvec": np.broadcast_to(
            np.array([EPS, ISQ, 0.0, 0.0, 0.0, 0.0, 0.0, 0.0],
                     dtype=np.float32)[None, :], (128, 8)).copy(),
    }
    j = np.arange(S, dtype=np.float32)
    base["jrowi"] = np.broadcast_to(j[None, :], (128, S)).copy()
    slopes = np.array(_alibi_slopes(H), dtype=np.float32)
    base["biasbi"] = np.broadcast_to(
        (slopes[:, None] * j[None, :])[:, None, :], (H, 128, S)).copy()
    base["dscb"] = np.broadcast_to(
        np.asarray(inputs["L0_ffn_down_scale"], dtype=np.float32)[None, :],
        (128, D)).copy()
    jl = np.arange(128, dtype=np.float32)
    base["cmaski"] = np.where(jl[None, :] > jl[:, None], np.float32(NEG),
                              np.float32(0.0)).astype(np.float32)
    return x, base


def kernel(**inputs):
    x, base = _prep_inputs(inputs)
    nc = build_program()
    in_maps = []
    for c in range(NC):
        m = dict(base)
        m["x"] = np.ascontiguousarray(
            x[c * BL:(c + 1) * BL].reshape(T, D))
        in_maps.append(m)
    res = run_bass_kernel_spmd(nc, in_maps, core_ids=list(range(NC)))
    outs = [np.asarray(r["out"], dtype=np.float32).reshape(BL, S, D)
            for r in res.results]
    return np.concatenate(outs, axis=0)


# version 31
# speedup vs baseline: 1.0245x; 1.0245x over previous
import math
from collections import defaultdict
from contextlib import ExitStack

import numpy as np
import ml_dtypes
import concourse.bass as bass
import concourse.mybir as mybir
from concourse.bass_utils import run_bass_kernel_spmd

F32 = mybir.dt.float32
BF16 = mybir.dt.bfloat16
I32 = mybir.dt.int32
AX = mybir.AxisListType.X
ALU = mybir.AluOpType
AF = mybir.ActivationFunctionType

B, S, D, H, DH = 16, 2048, 2048, 16, 128
NC = 8
BL = B // NC
T = BL * S
I = 8192
DC = D // 128
ICN = I // 128
TT = T // 128
TB = T // 512
EPS = 1e-5
ISQ = 1.0 / math.sqrt(DH)
NEG = -1.0e30


def _alibi_slopes(n):
    def pow2(m):
        start = 2 ** (-(2 ** (-(math.log2(m) - 3))))
        return [start * start ** i for i in range(m)]
    if math.log2(n).is_integer():
        return pow2(n)
    c = 2 ** math.floor(math.log2(n))
    return pow2(c) + _alibi_slopes(2 * c)[0::2][: n - c]


class Em:
    def __init__(self):
        self.streams = defaultdict(list)
        self.C = defaultdict(int)
        self.waited = defaultdict(dict)
        self.sems = {}

    def wait(self, eng, sem, val):
        if val <= 0:
            return
        if self.waited[eng].get(sem, -1) >= val:
            return
        self.waited[eng][sem] = val
        self.streams[eng].append(
            lambda e, s=sem, v=val: e.wait_ge(self.sems[s], v))

    def op(self, eng, sem, waits, f, n=1):
        for s, v in waits:
            self.wait(eng, s, v)
        self.C[sem] += n
        cnt = self.C[sem]
        self.streams[eng].append(
            lambda e, f=f, s=sem, n=n: f(e).then_inc(self.sems[s], n))
        return cnt

    def dma(self, eng, sem, waits, out_ap, in_ap):
        for s, v in waits:
            self.wait(eng, s, v)
        self.C[sem] += 16
        cnt = self.C[sem]
        self.streams[eng].append(
            lambda e, o=out_ap, i=in_ap, s=sem: e.dma_start(out=o, in_=i)
            .then_inc(self.sems[s], 16))
        return cnt

    def raw(self, eng, f):
        self.streams[eng].append(f)

    def replay(self, block):
        order = [("sync", block.sync), ("tensor", block.tensor),
                 ("vector", block.vector), ("scalar", block.scalar),
                 ("gpsimd", block.gpsimd)]
        for name, starter in order:
            if not self.streams[name]:
                continue
            fns = self.streams[name]

            def body(e, fns=fns):
                for fn in fns:
                    fn(e)
            starter(body)


def build_program(dbg=(), upto=3):
    nc = bass.Bass("TRN2", target_bir_lowering=False)

    def dram(name, shape, dt, kind):
        k = "ExternalOutput" if name in dbg and kind == "Internal" else kind
        return nc.dram_tensor(name, shape, dt, kind=k).ap()

    x = dram("x", [T, D], F32, "ExternalInput")
    wqT = dram("wqT", [D, D], F32, "ExternalInput")
    wkT = dram("wkT", [D, D], F32, "ExternalInput")
    wvT = dram("wvT", [D, D], F32, "ExternalInput")
    woT = dram("woT", [D, D], F32, "ExternalInput")
    wgT = dram("wgT", [D, I], I32, "ExternalInput")
    wuT = dram("wuT", [D, I], I32, "ExternalInput")
    wdT = dram("wdT", [I, D], I32, "ExternalInput")
    gsc = dram("gsc", [128, ICN], F32, "ExternalInput")
    usc = dram("usc", [128, ICN], F32, "ExternalInput")
    dsc = dram("dsc", [128, DC], F32, "ExternalInput")
    idbf = dram("idbf", [128, 128], BF16, "ExternalInput")
    idf32 = dram("idf32", [128, 128], F32, "ExternalInput")
    jrowi = dram("jrowi", [128, S], F32, "ExternalInput")
    biasbi = dram("biasbi", [H, 128, S], F32, "ExternalInput")
    dscb = dram("dscb", [128, D], F32, "ExternalInput")
    cvec = dram("cvec", [128, 8], F32, "ExternalInput")
    cmaski = dram("cmaski", [128, 128], F32, "ExternalInput")

    out = dram("out", [T, D], F32, "ExternalOutput")

    qs = dram("qs", [H, DH, T], BF16, "Internal")
    ks = dram("ks", [H, DH, T], BF16, "Internal")
    vs = dram("vs", [H, DH, T], BF16, "Internal")
    x1s = dram("x1s", [T, D], F32, "Internal")
    xn2t = dram("xn2t", [TT, 128, D], BF16, "Internal")
    wgbf = dram("wgbf", [ICN, 128, DC, 128], BF16, "Internal")
    wubf = dram("wubf", [ICN, 128, DC, 128], BF16, "Internal")
    wdbf = dram("wdbf", [ICN, 128, D], BF16, "Internal")
    wobf = dram("wobf", [D, D], BF16, "Internal")

    em = Em()

    with ExitStack() as ctx:
        def sb(name, shape, dt):
            return ctx.enter_context(nc.sbuf_tensor(name, shape, dt))

        def psum(name, shape, dt):
            return ctx.enter_context(nc.psum_tensor(name, shape, dt))

        def sem(name):
            em.sems[name] = ctx.enter_context(nc.semaphore(name))
            return name

        for s_ in ["sv", "sa", "sp", "sg",
                   "c_ld",
                   "sx0", "sx1",
                   "swt0", "swt1", "swt2",
                   "sqst0", "sqst1",
                   "sfw0", "sfw1", "sfw2", "sfw3", "sfw4", "sfw5",
                   "sfs0", "sfs1", "sfs2", "sfs3", "sfs4", "sfs5",
                   "skv0", "skv1",
                   "sbs0", "sbs1",
                   "swo0", "swo1",
                   "sxr0", "sxr1",
                   "sx1st0", "sx1st1",
                   "sx1r0", "sx1r1",
                   "sxnst0", "sxnst1",
                   "sxn0", "sxn1",
                   "sgu0", "sgu1", "sgu2",
                   "swd0", "swd1", "swd2", "swd3",
                   "sost",
                   ]:
            sem(s_)

        id_bf = sb("id_bf", [128, 128], BF16)
        cstg = [sb(f"cstg{i}", [128, 512], I32) for i in range(6)]
        cbf = [sb(f"cbf{i}", [128, 512], BF16) for i in range(6)]
        id_f32 = sb("id_f32", [128, 128], F32)
        gsc_sb = sb("gsc_sb", [128, ICN], F32)
        usc_sb = sb("usc_sb", [128, ICN], F32)
        dsc_sb = sb("dsc_sb", [128, DC], F32)
        cv = sb("cv", [128, 8], F32)

        for dst, src in [(id_bf, idbf), (id_f32, idf32), (gsc_sb, gsc),
                         (usc_sb, usc), (dsc_sb, dsc), (cv, cvec)]:
            em.dma("sync", "c_ld", [], dst[:], src)
        CLD = em.C["c_ld"]

        cast_jobs = []
        for rb in range(DC):
            for cc in range(4):
                cast_jobs.append(("f32", woT, wobf, rb, cc))
        WO_TILES = len(cast_jobs)
        for csrc, cdst, nrb, ncc, kind in (
                (wgT, wgbf, DC, I // 512, "gu"),
                (wuT, wubf, DC, I // 512, "gu"),
                (wdT, wdbf, ICN, D // 512, "wd")):
            for rb in range(nrb):
                for cc in range(ncc):
                    cast_jobs.append((kind, csrc, cdst, rb, cc))
        cast_state = {"ji": 0, "ld": 0, "done": [0] * 6,
                      "ldc": [0] * 6, "stc": [0] * 6, "wo": [0] * 6}

        def emit_casts(k):
            st_ = cast_state

            def issue_load():
                ji = st_["ld"]
                if ji >= len(cast_jobs):
                    return
                kind, csrc, cdst, rb, cc = cast_jobs[ji]
                sl = ji % 6
                stg = (cstg[sl][:] if kind in ("gu", "wd")
                       else cstg[sl][:].bitcast(F32))
                em.dma("scalar", f"sfw{sl}", [("sa", st_["done"][sl])],
                       stg, csrc[rb * 128:(rb + 1) * 128,
                                 cc * 512:(cc + 1) * 512])
                st_["ldc"][sl] = em.C[f"sfw{sl}"]
                st_["ld"] = ji + 1

            while st_["ld"] < min(st_["ji"] + 6, len(cast_jobs)):
                issue_load()
            for _ in range(k):
                ji = st_["ji"]
                if ji >= len(cast_jobs):
                    return
                kind, csrc, cdst, rb, cc = cast_jobs[ji]
                sl = ji % 6
                if kind in ("gu", "wd"):
                    st_["done"][sl] = em.op(
                        "scalar", "sa",
                        [(f"sfw{sl}", st_["ldc"][sl]),
                         (f"sfs{sl}", st_["stc"][sl])],
                        lambda e, sl=sl: e.activation(
                            out=cbf[sl][:], in_=cstg[sl][:],
                            func=AF.Copy, bias=-128.0, scale=1.0))
                else:
                    st_["done"][sl] = em.op(
                        "scalar", "sa",
                        [(f"sfw{sl}", st_["ldc"][sl]),
                         (f"sfs{sl}", st_["stc"][sl])],
                        lambda e, sl=sl: e.activation(
                            out=cbf[sl][:], in_=cstg[sl][:].bitcast(F32),
                            func=AF.Copy))
                if kind == "gu":
                    dstap = (cdst[4 * cc:4 * cc + 4, :, rb, :]
                             .rearrange("j p i -> p j i"))
                    srcap = cbf[sl][:].rearrange("p (j i) -> p j i", i=128)
                elif kind == "wd":
                    dstap = cdst[rb][:, cc * 512:(cc + 1) * 512]
                    srcap = cbf[sl][:]
                else:
                    dstap = cdst[rb * 128:(rb + 1) * 128,
                                 cc * 512:(cc + 1) * 512]
                    srcap = cbf[sl][:]
                st_["stc"][sl] = em.dma(
                    "scalar", f"sfs{sl}", [("sa", st_["done"][sl])],
                    dstap, srcap)
                if ji == WO_TILES - 1:
                    st_["wo"] = [em.C[f"sfs{i}"] for i in range(6)]
                st_["ji"] = ji + 1
                if st_["ld"] < len(cast_jobs):
                    issue_load()


        with ExitStack() as p1:
            def sb1(name, shape, dt):
                return p1.enter_context(nc.sbuf_tensor(name, shape, dt))
            xts = [sb1(f"xts{i}", [128, D], F32) for i in range(2)]
            xnbf = [sb1(f"xnbf{i}", [128, D], BF16) for i in range(2)]
            stat = sb1("stat", [128, 16], F32)
            xnfm = sb1("xnfm", [128, DC * T], BF16)
            wstage = [sb1(f"wstage{i}", [128, D], F32) for i in range(2)]
            wbf = [sb1(f"wbf{i}", [128, D], BF16) for i in range(2)]
            qstage = [sb1(f"qstage{i}", [128, 512], BF16) for i in range(2)]
            ln_ps = [p1.enter_context(nc.psum_tensor(f"lnps{i}", [128, 1024], BF16))
                     for i in range(2)]
            qkv_ps = [p1.enter_context(nc.psum_tensor(f"qkvps{i}", [128, 512], F32))
                      for i in range(6)]

            def st(slot, j):
                return stat[:, slot * 8 + j: slot * 8 + j + 1]

            norm_cnt = [0, 0]
            evac_cnt = [0, 0]
            ln_tile_done = [0] * TT
            ld_n = [0, 0]
            ln_tp = [0, 0]
            st1_saved = {}
            st2_saved = {}
            sq_saved = {}

            def ln_load(tt):
                sl = tt % 2
                ld_n[sl] = em.dma("sync", f"sx{sl}", [("sv", norm_cnt[sl])],
                                  xts[sl][:], x[tt * 128:(tt + 1) * 128, :])

            def ln_stage1(tt):
                sl = tt % 2
                c1 = em.op("vector", "sv", [(f"sx{sl}", ld_n[sl])],
                           lambda e, sl=sl: e.tensor_reduce(
                               out=st(sl, 0), in_=xts[sl][:], axis=AX,
                               op=ALU.add))
                em.op("vector", "sv", [("sv", c1)],
                      lambda e, sl=sl: e.tensor_scalar(
                          out=st(sl, 0), in0=st(sl, 0), scalar1=1.0 / D,
                          scalar2=None, op0=ALU.mult))
                sq = em.op("scalar", "sa", [(f"sx{sl}", ld_n[sl]),
                                            ("sp", ln_tp[sl])],
                           lambda e, sl=sl: e.activation(
                               out=xnbf[sl][:], in_=xts[sl][:],
                               func=AF.Square, accum_out=st(sl, 1)))
                sq_saved[tt] = sq
                c3 = em.op("vector", "sv", [("sv", c1 + 1)],
                           lambda e, sl=sl: e.tensor_scalar(
                               out=st(sl, 2), in0=st(sl, 0),
                               scalar1=st(sl, 0), scalar2=None, op0=ALU.mult))
                st1_saved[tt] = em.op(
                    "vector", "sv", [("sv", c3), ("sa", sq)],
                    lambda e, sl=sl: e.scalar_tensor_tensor(
                        out=st(sl, 3), in0=st(sl, 1), scalar=1.0 / D,
                        in1=st(sl, 2), op0=ALU.mult, op1=ALU.subtract))

            def ln_stage2(tt):
                sl = tt % 2
                em.wait("scalar", "c_ld", CLD)
                l1 = em.op("scalar", "sa", [("sv", st1_saved[tt])],
                           lambda e, sl=sl: e.activation(
                               out=st(sl, 4), in_=st(sl, 3), func=AF.Ln,
                               bias=cv[:, 0:1], scale=1.0))
                st2_saved[tt] = em.op(
                    "scalar", "sa", [("sa", l1)],
                    lambda e, sl=sl: e.activation(
                        out=st(sl, 5), in_=st(sl, 4), func=AF.Exp,
                        bias=0.0, scale=-0.5))

            def ln_stage3(tt):
                sl = tt % 2
                norm_cnt[sl] = em.op(
                    "vector", "sv", [("sa", st2_saved[tt])],
                    lambda e, sl=sl: e.tensor_scalar(
                        out=xnbf[sl][:], in0=xts[sl][:], scalar1=st(sl, 0),
                        scalar2=st(sl, 5), op0=ALU.subtract, op1=ALU.mult))
                em.wait("tensor", "c_ld", CLD)
                em.wait("tensor", "sv", norm_cnt[sl])
                for half in range(2):
                    em.wait("tensor", "sv", evac_cnt[half])
                    tp = 0
                    for k in range(8):
                        dc = half * 8 + k
                        tp = em.op("tensor", "sp", [],
                                   lambda e, sl=sl, dc=dc, k=k, half=half:
                                   e.transpose(
                                       out=ln_ps[half][:, k * 128:(k + 1) * 128],
                                       in_=xnbf[sl][:, dc * 128:(dc + 1) * 128],
                                       identity=id_bf[:]))
                    evac_cnt[half] = em.op(
                        "vector", "sv", [("sp", tp)],
                        lambda e, half=half, tt=tt: e.tensor_copy(
                            out=xnfm[:].rearrange("p (dc t) -> p dc t", dc=DC)
                            [:, half * 8:half * 8 + 8, tt * 128:(tt + 1) * 128],
                            in_=ln_ps[half][:].rearrange(
                                "p (dc t) -> p dc t", t=128)))
                    ln_tp[sl] = tp
                ln_tile_done[tt] = evac_cnt[1]

            wT_srcs = [(wqT, qs), (wkT, ks), (wvT, vs)]
            cast_cnt = [0, 0]
            mm_last_for_wbf = [0, 0]
            mm_last_for_wstage = [0, 0]
            evq_cnt = [0] * 6
            evq_eng = [None] * 6
            qst_cnt = [0, 0]
            n_ev = 0
            NW = 3 * DC
            wload_cnt = [0, 0]

            def w_load(gi):
                pi_, oc_ = gi // DC, gi % DC
                sl = gi % 2
                wload_cnt[sl] = em.dma(
                    "sync", f"swt{sl}", [("sv", cast_cnt[sl])],
                    wstage[sl][:].rearrange("p (dc o) -> p dc o", o=128),
                    wT_srcs[pi_][0][:, oc_ * 128:(oc_ + 1) * 128]
                    .rearrange("(dc p) o -> p dc o", p=128))

            def w_cast(gi):
                sl = gi % 2
                cast_cnt[sl] = em.op(
                    "vector", "sv",
                    [(f"swt{sl}", wload_cnt[sl]), ("sp", mm_last_for_wbf[sl])],
                    lambda e, sl=sl: e.tensor_copy(
                        out=wbf[sl][:], in_=wstage[sl][:]))

            ln_load(0)
            w_load(0)
            w_cast(0)
            w_load(1)
            for tt in range(TT + 1):
                emit_casts(3)
                if tt >= 1:
                    ln_stage3(tt - 1)
                if tt < TT:
                    if tt + 1 < TT:
                        ln_load(tt + 1)
                    ln_stage1(tt)
                    ln_stage2(tt)
            LN_DONE = em.C["sv"]

            for gi in range(NW):
                pi, oc = gi // DC, gi % DC
                dst = wT_srcs[pi][1]
                sl = gi % 2
                if gi + 1 < NW:
                    w_cast(gi + 1)
                if gi + 2 < NW:
                    w_load(gi + 2)
                em.wait("tensor", "sv", cast_cnt[sl])
                for tbg, tbs in enumerate(((0,), (1, 2, 3), (4, 5), (6, 7))):
                    bset = (tbg % 2) * 3
                    for j in range(len(tbs)):
                        bk = bset + j
                        if evq_eng[bk] is not None:
                            em.wait("tensor", evq_eng[bk], evq_cnt[bk])
                    em.wait("tensor", "sv",
                            ln_tile_done[tbs[-1] * 4 + 3])
                    mmn = 0
                    for dc in range(DC):
                        for j, tb in enumerate(tbs):
                            bk = bset + j
                            mmn = em.op(
                                "tensor", "sp", [],
                                lambda e, sl=sl, dc=dc, bk=bk, tb=tb:
                                e.matmul(
                                    out=qkv_ps[bk][:],
                                    lhsT=wbf[sl][:, dc * 128:(dc + 1) * 128],
                                    rhs=xnfm[:, dc * T + tb * 512:
                                             dc * T + tb * 512 + 512],
                                    start=(dc == 0), stop=(dc == DC - 1)))
                    mm_last_for_wbf[sl] = mmn
                    for j, tb in enumerate(tbs):
                        bk = bset + j
                        ssl = n_ev % 2
                        ssem = f"sqst{ssl}"
                        if pi == 0:
                            ev = em.op("vector", "sv",
                                       [("sp", mmn), (ssem, qst_cnt[ssl])],
                                       lambda e, bk=bk, ssl=ssl:
                                       e.tensor_scalar(
                                           out=qstage[ssl][:],
                                           in0=qkv_ps[bk][:],
                                           scalar1=ISQ, scalar2=None,
                                           op0=ALU.mult))
                        else:
                            ev = em.op("vector", "sv",
                                       [("sp", mmn), (ssem, qst_cnt[ssl])],
                                       lambda e, bk=bk, ssl=ssl:
                                       e.tensor_copy(
                                           out=qstage[ssl][:],
                                           in_=qkv_ps[bk][:]))
                        evq_cnt[bk], evq_eng[bk] = ev, "sv"
                        qst_cnt[ssl] = em.dma(
                            "sync", ssem, [("sv", ev)],
                            dst[oc][:, tb * 512: tb * 512 + 512],
                            qstage[ssl][:])
                        n_ev += 1
                    emit_casts(6)
            QKV_ST = (em.C["sqst0"], em.C["sqst1"])

        def barrier(engines=("sync", "tensor", "vector", "scalar"),
                    dma_sems=()):
            targets = [(s_, em.C[s_]) for s_ in ("sv", "sa", "sp")]
            targets += [(s_, em.C[s_]) for s_ in dma_sems]
            for eng in engines:
                for s_, v_ in targets:
                    em.wait(eng, s_, v_)

        ALLE = ("sync", "tensor", "vector", "scalar", "gpsimd")
        barrier(engines=ALLE, dma_sems=("sqst0", "sqst1"))
        em.wait("gpsimd", "sg", em.C["sg"])

        if upto < 2:
            em.wait("sync", "sqst0", QKV_ST[0])
            em.wait("sync", "sqst1", QKV_ST[1])
            with nc.Block() as block:
                em.replay(block)
            return nc

        with ExitStack() as p2:
            def sb2(name, shape, dt):
                return p2.enter_context(nc.sbuf_tensor(name, shape, dt))
            kh = [sb2(f"kh{i}", [128, S], BF16) for i in range(2)]
            qh = [sb2(f"qh{i}", [128, S], BF16) for i in range(2)]
            vh = [sb2(f"vh{i}", [128, S], BF16) for i in range(2)]
            vt = [sb2(f"vt{i}", [128, S], BF16) for i in range(2)]
            sf = [sb2(f"sf{i}", [128, S], F32) for i in range(3)]
            pb = [sb2(f"pb{i}", [128, S], BF16) for i in range(3)]
            pt = [sb2(f"pt{i}", [128, S], BF16) for i in range(3)]
            biash = [sb2(f"biash{i}", [128, S], F32) for i in range(2)]
            cmask = sb2("cmask", [128, 128], F32)
            Zb = sb2("Zb", [128, H * S], BF16)
            st2 = sb2("st2", [128, 32], F32)
            wo_sb = [sb2(f"wo_sb{i}", [128, DC * 256], BF16) for i in range(2)]
            xres = [sb2(f"xres{i}", [128, 256], F32) for i in range(2)]
            x1st = [sb2(f"x1st{i}", [128, 256], F32) for i in range(2)]
            sc_ps = [p2.enter_context(nc.psum_tensor(f"scps{i}", [128, 512], F32))
                     for i in range(4)]
            tp_ps = [p2.enter_context(nc.psum_tensor(f"tpps{i}", [128, 1024], BF16))
                     for i in range(2)]
            z_ps = [p2.enter_context(nc.psum_tensor(f"zps{i}", [128, 128], F32))
                    for i in range(2)]
            o_ps = sc_ps

            em.dma("sync", "c_ld", [], cmask[:], cmaski)
            CLD2 = em.C["c_ld"]

            slot_war = [0, 0]
            tp_war = [("sa", 0), ("sa", 0)]
            p1_war = [0, 0, 0, 0]
            kv_ld = [0, 0]
            slopes = _alibi_slopes(H)
            ccg = 0

            def stv(j):
                return st2[:, j:j + 1]

            def emit_head_loads(s, h):
                sl = (s * H + h) % 2
                sem_ = f"skv{sl}"
                for buf, src_ in ((kh, ks), (qh, qs), (vh, vs)):
                    em.dma("sync", sem_, [("sp", em.C["sp"])],
                           buf[sl][:], src_[h][:, s * S:(s + 1) * S])
                kv_ld[sl] = em.C[sem_]

            for s in range(BL):
                if s == 0:
                    emit_head_loads(s, 0)
                    emit_head_loads(s, 1)
                NCH = H * 16
                p1l = {}
                negl = {}
                expl = {}
                recl = {}
                pscl = {}
                egrp = {}
                flast = {}
                gmm = {}
                hl = {}
                vtr = {}
                bias_war = [0, 0]

                def A_(i):
                    nonlocal ccg
                    h, qt = i // 16, i % 16
                    sl = (s * H + h) % 2
                    qp = i % 3
                    kend = (qt + 1) * 128
                    qsl = qh[sl][:, qt * 128:(qt + 1) * 128]
                    for c in range((kend + 511) // 512):
                        N = min(512, kend - c * 512)
                        pp = ccg % 4
                        mmn = em.op(
                            "tensor", "sp",
                            [("sa", p1_war[pp]), (f"skv{sl}", kv_ld[sl])],
                            lambda e, pp=pp, c=c, N=N, sl=sl, qsl=qsl:
                            e.matmul(out=sc_ps[pp][:, :N], lhsT=qsl,
                                     rhs=kh[sl][:, c * 512: c * 512 + N],
                                     start=True, stop=True))
                        p1_war[pp] = em.op(
                            "scalar", "sa",
                            [("sp", mmn), ("sa", expl.get(i - 3, 0))],
                            lambda e, pp=pp, c=c, N=N, qp=qp: e.activation(
                                out=sf[qp][:, c * 512: c * 512 + N],
                                in_=sc_ps[pp][:, :N], func=AF.Copy))
                        p1l[i] = p1_war[pp]
                        ccg += 1

                def TT_(i):
                    h, qt = i // 16, i % 16
                    qp = i % 3
                    kend = (qt + 1) * 128
                    em.wait("vector", "c_ld", CLD2)
                    c1 = em.op("vector", "sv", [("sa", p1l[i])],
                               lambda e, qp=qp, kend=kend: e.tensor_scalar(
                                   out=sf[qp][:, :kend], in0=sf[qp][:, :kend],
                                   scalar1=1024.0, scalar2=-1024.0,
                                   op0=ALU.min, op1=ALU.max))
                    c2 = em.op("vector", "sv", [("sv", c1)],
                               lambda e, qp=qp, qt=qt: e.tensor_tensor(
                                   out=sf[qp][:, qt * 128:(qt + 1) * 128],
                                   in0=sf[qp][:, qt * 128:(qt + 1) * 128],
                                   in1=cmask[:], op=ALU.add))
                    c3 = em.op("vector", "sv",
                               [("sv", c2), (f"sbs{h % 2}", bias_war[h % 2])],
                               lambda e, qp=qp, kend=kend, h=h: e.tensor_tensor(
                                   out=sf[qp][:, :kend], in0=sf[qp][:, :kend],
                                   in1=biash[h % 2][:, :kend], op=ALU.add))
                    negl[i] = em.op(
                        "vector", "sv", [("sv", c3)],
                        lambda e, qp=qp, kend=kend: e.tensor_reduce(
                            out=stv(qp * 5 + 3), in_=sf[qp][:, :kend],
                            axis=AX, op=ALU.max, negate=True))

                def C_(i):
                    h, qt = i // 16, i % 16
                    qp = i % 3
                    kend = (qt + 1) * 128
                    expl[i] = em.op(
                        "scalar", "sa",
                        [("sv", negl[i]),
                         ("sp", egrp.get(i - 3, [(0, 0, 0, 0)])[-1][3]),
                         ("sv", recl.get(i - 3, 0))],
                        lambda e, qp=qp, kend=kend: e.activation(
                            out=pb[qp][:, :kend], in_=sf[qp][:, :kend],
                            func=AF.Exp, bias=stv(qp * 5 + 3), scale=1.0,
                            accum_out=stv(qp * 5 + 1)))

                def D_(i):
                    qp = i % 3
                    recl[i] = em.op("vector", "sv", [("sa", expl[i])],
                                    lambda e, qp=qp: e.reciprocal(
                                        out=stv(qp * 5 + 2),
                                        in_=stv(qp * 5 + 1)))

                def PS_(i):
                    qp = i % 3
                    kend = (i % 16 + 1) * 128
                    pscl[i] = em.op(
                        "vector", "sv", [("sv", recl[i])],
                        lambda e, qp=qp, kend=kend: e.tensor_scalar(
                            out=pb[qp][:, :kend], in0=pb[qp][:, :kend],
                            scalar1=stv(qp * 5 + 2), scalar2=None,
                            op0=ALU.mult))

                def E_(i):
                    qp = i % 3
                    kch = i % 16 + 1
                    groups = []
                    em.wait("tensor", "sv", pscl[i])
                    for g in range((kch + 7) // 8):
                        gb = g % 2
                        ntl = min(8, kch - g * 8)
                        em.wait("tensor", tp_war[gb][0], tp_war[gb][1])
                        tpn = 0
                        for k in range(ntl):
                            kc = g * 8 + k
                            tpn = em.op(
                                "tensor", "sp", [],
                                lambda e, qp=qp, kc=kc, gb=gb, k=k: e.transpose(
                                    out=tp_ps[gb][:, k * 128:(k + 1) * 128],
                                    in_=pb[qp][:, kc * 128:(kc + 1) * 128],
                                    identity=id_bf[:]))
                        groups.append((g, gb, ntl, tpn))
                    egrp[i] = groups

                def F_(i):
                    qp = i % 3
                    last = 0
                    for g, gb, ntl, tpn in egrp[i]:
                        last = em.op(
                            "scalar", "sa",
                            [("sp", tpn), ("sp", gmm.get(i - 3, 0))],
                            lambda e, qp=qp, g=g, gb=gb, ntl=ntl: e.activation(
                                out=pt[qp][:, g * 1024: g * 1024 + ntl * 128],
                                in_=tp_ps[gb][:, :ntl * 128], func=AF.Copy))
                        tp_war[gb] = ("sa", last)
                    flast[i] = last

                def G_(i):
                    h, qt = i // 16, i % 16
                    sl = (s * H + h) % 2
                    qp = i % 3
                    zp = i % 2
                    kch = qt + 1
                    em.wait("tensor", "sa", flast[i])
                    em.wait("tensor", "sv", vtr[h])
                    em.wait("tensor", "sv", hl.get(i - 2, 0))
                    mmn = 0
                    for kc in range(kch):
                        mmn = em.op(
                            "tensor", "sp", [],
                            lambda e, qp=qp, kc=kc, sl=sl, zp=zp, kch=kch:
                            e.matmul(
                                out=z_ps[zp][:],
                                lhsT=vt[sl][:, kc * 128:(kc + 1) * 128],
                                rhs=pt[qp][:, kc * 128:(kc + 1) * 128],
                                start=(kc == 0), stop=(kc == kch - 1)))
                    gmm[i] = mmn

                def H_(i):
                    h, qt = i // 16, i % 16
                    zp = i % 2
                    hl[i] = em.op(
                        "vector", "sv", [("sp", gmm[i])],
                        lambda e, zp=zp, h=h, qt=qt: e.tensor_copy(
                            out=Zb[:, h * S + qt * 128: h * S + (qt + 1) * 128],
                            in_=z_ps[zp][:]))

                def headstart(h):
                    sl = (s * H + h) % 2
                    emit_casts(1)
                    if h == 0:
                        bias_war[0] = em.dma(
                            "sync", "sbs0", [("sv", em.C["sv"])],
                            biash[0][:], biasbi[0])
                    em.wait("tensor", f"skv{sl}", kv_ld[sl])
                    vt_ready = 0
                    for g in range(2):
                        em.wait("tensor", tp_war[g][0], tp_war[g][1])
                        tpn = 0
                        for k in range(8):
                            kc = g * 8 + k
                            tpn = em.op(
                                "tensor", "sp", [],
                                lambda e, sl=sl, kc=kc, g=g, k=k: e.transpose(
                                    out=tp_ps[g][:, k * 128:(k + 1) * 128],
                                    in_=vh[sl][:, kc * 128:(kc + 1) * 128],
                                    identity=id_bf[:]))
                        vt_ready = em.op(
                            "vector", "sv",
                            [("sp", tpn), ("sp", slot_war[sl])],
                            lambda e, sl=sl, g=g: e.tensor_copy(
                                out=vt[sl][:, g * 1024:(g + 1) * 1024],
                                in_=tp_ps[g][:]))
                        tp_war[g] = ("sv", vt_ready)
                    vtr[h] = vt_ready
                    nh = s * H + h + 1
                    if nh < BL * H and nh >= 2:
                        emit_head_loads(nh // H, nh % H)

                for i in range(NCH + 5):
                    if i < NCH:
                        h, qt = i // 16, i % 16
                        if qt == 0:
                            headstart(h)
                        if qt == 1 and h + 1 < H:
                            bias_war[(h + 1) % 2] = em.dma(
                                "sync", f"sbs{(h + 1) % 2}",
                                [("sv", em.C["sv"])],
                                biash[(h + 1) % 2][:], biasbi[h + 1])
                        A_(i)
                    if 0 <= i - 1 < NCH:
                        TT_(i - 1)
                    if 0 <= i - 2 < NCH:
                        C_(i - 2)
                    if 0 <= i - 3 < NCH:
                        D_(i - 3)
                        PS_(i - 3)
                    if 0 <= i - 4 < NCH:
                        E_(i - 4)
                        F_(i - 4)
                    if 0 <= i - 5 < NCH:
                        G_(i - 5)
                        H_(i - 5)
                    if (i - 5) >= 0 and (i - 5) % 16 == 15:
                        hdone = (i - 5) // 16
                        slot_war[(s * H + hdone) % 2] = em.C["sp"]
                Z_DONE = em.C["sv"]

                barrier()
                emit_casts(150)
                wo_ld = [0, 0]
                ops_war = [0, 0]
                x1_evac = [0, 0]
                bank_evac = [0, 0, 0, 0]
                xr_ld = [0, 0]
                x1_stc = [0, 0]
                for ci in range(6):
                    em.wait("sync", f"sfs{ci}", cast_state["wo"][ci])

                def wo_load(dblk):
                    wsl = dblk % 2
                    wo_ld[wsl] = em.dma(
                        "sync", f"swo{wsl}", [("sp", ops_war[wsl])],
                        wo_sb[wsl][:].rearrange("p (oc d) -> p oc d", d=256),
                        wobf[:, dblk * 256:(dblk + 1) * 256]
                        .rearrange("(oc p) d -> p oc d", p=128))

                wo_load(0)
                for dblk in range(8):
                    wsl = dblk % 2
                    if dblk + 1 < 8:
                        wo_load(dblk + 1)
                    for tc in range(16):
                        pp4 = (dblk * 16 + tc) % 4
                        pp2 = (dblk * 16 + tc) % 2
                        xr_ld[pp2] = em.dma(
                            "sync", f"sxr{pp2}", [("sv", x1_evac[pp2])],
                            xres[pp2][:],
                            x[s * S + tc * 128: s * S + (tc + 1) * 128,
                              dblk * 256:(dblk + 1) * 256])
                        em.wait("tensor", f"swo{wsl}", wo_ld[wsl])
                        em.wait("tensor", "sv", bank_evac[pp4])
                        mmn = 0
                        for oc in range(DC):
                            mmn = em.op(
                                "tensor", "sp", [],
                                lambda e, pp4=pp4, oc=oc, tc=tc, wsl=wsl:
                                e.matmul(
                                    out=o_ps[pp4][:, :256],
                                    lhsT=Zb[:, oc * S + tc * 128:
                                            oc * S + (tc + 1) * 128],
                                    rhs=wo_sb[wsl][:, oc * 256:(oc + 1) * 256],
                                    start=(oc == 0), stop=(oc == DC - 1)))
                        ops_war[wsl] = mmn
                        ev_ = em.op(
                            "vector", "sv",
                            [("sp", mmn), (f"sxr{pp2}", xr_ld[pp2]),
                             (f"sx1st{pp2}", x1_stc[pp2])],
                            lambda e, pp4=pp4, pp2=pp2: e.tensor_tensor(
                                out=x1st[pp2][:], in0=o_ps[pp4][:, :256],
                                in1=xres[pp2][:], op=ALU.add))
                        x1_evac[pp2] = ev_
                        bank_evac[pp4] = ev_
                        x1_stc[pp2] = em.dma(
                            "sync", f"sx1st{pp2}", [("sv", ev_)],
                            x1s[s * S + tc * 128: s * S + (tc + 1) * 128,
                                dblk * 256:(dblk + 1) * 256],
                            x1st[pp2][:])

                barrier(dma_sems=("sx1st0", "sx1st1"))
                xn_stc = [0, 0]
                ln2_nrm = [0, 0]
                l2_tp = [0, 0]
                l2_ld = [0, 0]
                l2_s1 = {}
                l2_s2 = {}

                def l2_load(t16):
                    sl = t16 % 2
                    l2_ld[sl] = em.dma(
                        "sync", f"sx1r{sl}", [("sv", ln2_nrm[sl])],
                        sf[sl][:], x1s[(s * 16 + t16) * 128:
                                       (s * 16 + t16 + 1) * 128, :])

                def l2_stage1(t16):
                    sl = t16 % 2
                    c1 = em.op("vector", "sv", [(f"sx1r{sl}", l2_ld[sl])],
                               lambda e, sl=sl: e.tensor_reduce(
                                   out=stv(16 + sl * 6), in_=sf[sl][:],
                                   axis=AX, op=ALU.add))
                    em.op("vector", "sv", [("sv", c1)],
                          lambda e, sl=sl: e.tensor_scalar(
                              out=stv(16 + sl * 6), in0=stv(16 + sl * 6),
                              scalar1=1.0 / D, scalar2=None, op0=ALU.mult))
                    sq = em.op("scalar", "sa",
                               [(f"sx1r{sl}", l2_ld[sl]),
                                ("sp", l2_tp[sl])],
                               lambda e, sl=sl: e.activation(
                                   out=pb[sl][:], in_=sf[sl][:],
                                   func=AF.Square,
                                   accum_out=stv(17 + sl * 6)))
                    c3 = em.op("vector", "sv", [("sv", c1 + 1)],
                               lambda e, sl=sl: e.tensor_scalar(
                                   out=stv(18 + sl * 6), in0=stv(16 + sl * 6),
                                   scalar1=stv(16 + sl * 6), scalar2=None,
                                   op0=ALU.mult))
                    l2_s1[t16] = em.op(
                        "vector", "sv", [("sv", c3), ("sa", sq)],
                        lambda e, sl=sl: e.scalar_tensor_tensor(
                            out=stv(19 + sl * 6), in0=stv(17 + sl * 6),
                            scalar=1.0 / D, in1=stv(18 + sl * 6),
                            op0=ALU.mult, op1=ALU.subtract))

                def l2_stage2(t16):
                    sl = t16 % 2
                    l1 = em.op("scalar", "sa", [("sv", l2_s1[t16])],
                               lambda e, sl=sl: e.activation(
                                   out=stv(20 + sl * 6), in_=stv(19 + sl * 6),
                                   func=AF.Ln, bias=cv[:, 0:1], scale=1.0))
                    l2_s2[t16] = em.op(
                        "scalar", "sa", [("sa", l1)],
                        lambda e, sl=sl: e.activation(
                            out=stv(21 + sl * 6), in_=stv(20 + sl * 6),
                            func=AF.Exp, bias=0.0, scale=-0.5))

                def l2_stage3(t16):
                    sl = t16 % 2
                    tt = s * 16 + t16
                    nrm = em.op("vector", "sv", [("sa", l2_s2[t16])],
                                lambda e, sl=sl: e.tensor_scalar(
                                    out=pb[sl][:], in0=sf[sl][:],
                                    scalar1=stv(16 + sl * 6),
                                    scalar2=stv(21 + sl * 6),
                                    op0=ALU.subtract, op1=ALU.mult))
                    ln2_nrm[sl] = nrm
                    em.wait("tensor", "sv", nrm)
                    evl = 0
                    for g in range(2):
                        em.wait("tensor", tp_war[g][0], tp_war[g][1])
                        tpn = 0
                        for k in range(8):
                            dc = g * 8 + k
                            tpn = em.op("tensor", "sp", [],
                                        lambda e, sl=sl, dc=dc, g=g, k=k:
                                        e.transpose(
                                            out=tp_ps[g][:, k * 128:(k + 1) * 128],
                                            in_=pb[sl][:, dc * 128:(dc + 1) * 128],
                                            identity=id_bf[:]))
                        evl = em.op("scalar", "sa",
                                    [("sp", tpn),
                                     (f"sxnst{sl}", xn_stc[sl])],
                                    lambda e, sl=sl, g=g: e.activation(
                                        out=pt[sl][:, g * 1024:(g + 1) * 1024],
                                        in_=tp_ps[g][:], func=AF.Copy))
                        tp_war[g] = ("sa", evl)
                        l2_tp[sl] = tpn
                    xn_stc[sl] = em.dma("sync", f"sxnst{sl}", [("sa", evl)],
                                        xn2t[tt], pt[sl][:])

                l2_load(0)
                for t16 in range(17):
                    if t16 >= 1:
                        l2_stage3(t16 - 1)
                    if t16 < 16:
                        if t16 + 1 < 16:
                            l2_load(t16 + 1)
                        l2_stage1(t16)
                        l2_stage2(t16)
                emit_casts(150)
                barrier(dma_sems=("sxnst0", "sxnst1"))

            emit_casts(len(cast_jobs))
            CAST_ST = tuple(em.C[f"sfs{i}"] for i in range(6))
            X1_ST = (em.C["sx1st0"], em.C["sx1st1"])
            XN_ST = (em.C["sxnst0"], em.C["sxnst1"])

        if upto < 3:
            for s_, v_ in ([("sx1st0", X1_ST[0]), ("sx1st1", X1_ST[1]),
                            ("sxnst0", XN_ST[0]), ("sxnst1", XN_ST[1]),
                            ("sqst0", QKV_ST[0]), ("sqst1", QKV_ST[1])] +
                           [(f"sfs{i}", CAST_ST[i]) for i in range(6)]):
                em.wait("sync", s_, v_)
            with nc.Block() as block:
                em.replay(block)
            return nc

        barrier(engines=ALLE,
                dma_sems=("sxnst0", "sxnst1", "sx1st0", "sx1st1", "sfs0",
                          "sfs1", "sfs2", "sfs3", "sfs4", "sfs5",
                          "sqst0", "sqst1"))
        em.wait("gpsimd", "sg", em.C["sg"])
        with ExitStack() as p3:
            def sb3(name, shape, dt):
                return p3.enter_context(nc.sbuf_tensor(name, shape, dt))
            xn2_sb = [sb3(f"xn2sb{i}", [128, DC * 512], BF16) for i in range(2)]
            wg_sb = [sb3(f"wgsb{i}", [128, DC * 128], BF16) for i in range(3)]
            wu_sb = [sb3(f"wusb{i}", [128, DC * 128], BF16) for i in range(3)]
            h_sb = sb3("hsb", [128, ICN * 512], BF16)
            wd_sb = [sb3(f"wdsb{i}", [128, 1024], BF16) for i in range(4)]
            dscb_sb = sb3("dscb_sb", [128, D], F32)
            x1t = [sb3(f"x1t{i}", [128, D], F32) for i in range(2)]
            ost = [sb3(f"ost{i}", [128, D], F32) for i in range(2)]
            ps8 = [p3.enter_context(nc.psum_tensor(f"ps8_{i}", [128, 512], F32))
                   for i in range(8)]
            em.dma("sync", "c_ld", [], dscb_sb[:], dscb)
            CLD3 = em.C["c_ld"]
            gu_ps = ps8

            xn_ld = [0, 0]
            gu_ld = [0, 0, 0]
            wd_ld = [0, 0, 0, 0]
            x1_ld = [0, 0]
            x1t_war = [0, 0]
            gu_mm_last = [0, 0, 0]
            xn_mm_last = [0, 0]
            wd_mm_last = [0, 0, 0, 0]
            bank_war = [("sv", 0)] * 8
            down_last = 0
            ostc = [0, 0]
            for tb in range(TB):
                xsl = tb % 2
                xn_ld[xsl] = em.dma(
                    "sync", f"sxn{xsl}", [("sp", xn_mm_last[xsl])],
                    xn2_sb[xsl][:].rearrange("p (dc tt c) -> p dc tt c",
                                             dc=DC, tt=4),
                    xn2t[tb * 4:(tb + 1) * 4]
                    .rearrange("tt p (dc c) -> p dc tt c", c=128))
                for ic in range(ICN):
                    sl = ic % 3
                    sgu = f"sgu{sl}"
                    em.dma("gpsimd", sgu, [("sp", gu_mm_last[sl])],
                           wg_sb[sl][:],
                           wgbf[ic].rearrange("p dc i -> p (dc i)"))
                    gu_ld[sl] = em.dma(
                        "gpsimd", sgu, [],
                        wu_sb[sl][:],
                        wubf[ic].rearrange("p dc i -> p (dc i)"))
                    em.wait("tensor", sgu, gu_ld[sl])
                    em.wait("tensor", f"sxn{xsl}", xn_ld[xsl])
                    em.wait("tensor", bank_war[2 * sl][0], bank_war[2 * sl][1])
                    em.wait("tensor", bank_war[2 * sl + 1][0],
                            bank_war[2 * sl + 1][1])
                    g_mm = u_mm = 0
                    for dc in range(DC):
                        g_mm = em.op(
                            "tensor", "sp", [],
                            lambda e, sl=sl, dc=dc, xsl=xsl: e.matmul(
                                out=gu_ps[2 * sl][:],
                                lhsT=wg_sb[sl][:, dc * 128:(dc + 1) * 128],
                                rhs=xn2_sb[xsl][:, dc * 512:(dc + 1) * 512],
                                start=(dc == 0), stop=(dc == DC - 1)))
                    for dc in range(DC):
                        u_mm = em.op(
                            "tensor", "sp", [],
                            lambda e, sl=sl, dc=dc, xsl=xsl: e.matmul(
                                out=gu_ps[2 * sl + 1][:],
                                lhsT=wu_sb[sl][:, dc * 128:(dc + 1) * 128],
                                rhs=xn2_sb[xsl][:, dc * 512:(dc + 1) * 512],
                                start=(dc == 0), stop=(dc == DC - 1)))
                    gu_mm_last[sl] = u_mm
                    xn_mm_last[xsl] = u_mm
                    sv_ = em.op("scalar", "sa",
                                [("sp", g_mm), ("sp", down_last)],
                                lambda e, sl=sl, ic=ic: e.activation(
                                    out=h_sb[:, ic * 512:(ic + 1) * 512],
                                    in_=gu_ps[2 * sl][:], func=AF.Silu,
                                    scale=gsc_sb[:, ic:ic + 1]))
                    bank_war[2 * sl] = ("sa", sv_)
                    hm = em.op(
                        "vector", "sv", [("sp", u_mm), ("sa", sv_)],
                        lambda e, sl=sl, ic=ic: e.scalar_tensor_tensor(
                            out=h_sb[:, ic * 512:(ic + 1) * 512],
                            in0=gu_ps[2 * sl + 1][:],
                            scalar=usc_sb[:, ic:ic + 1],
                            in1=h_sb[:, ic * 512:(ic + 1) * 512],
                            op0=ALU.mult, op1=ALU.mult))
                    bank_war[2 * sl + 1] = ("sv", hm)
                H_DONE = em.C["sv"]
                em.wait("vector", "c_ld", CLD3)
                for dhh in range(2):
                    for ic in range(ICN):
                        sl = ic % 4
                        swd = f"swd{sl}"
                        wd_ld[sl] = em.dma(
                            "scalar", swd, [("sp", wd_mm_last[sl])],
                            wd_sb[sl][:],
                            wdbf[ic][:, dhh * 1024:(dhh + 1) * 1024])
                        em.wait("tensor", swd, wd_ld[sl])
                        if ic == 0:
                            em.wait("tensor", "sv", H_DONE)
                        mmn = 0
                        for tc in range(4):
                            for db in range(2):
                                if ic == 0:
                                    em.wait("tensor", bank_war[tc * 2 + db][0],
                                            bank_war[tc * 2 + db][1])
                                mmn = em.op(
                                    "tensor", "sp", [],
                                    lambda e, sl=sl, ic=ic, tc=tc,
                                    db=db: e.matmul(
                                        out=ps8[tc * 2 + db][:],
                                        lhsT=h_sb[:, ic * 512 + tc * 128:
                                                  ic * 512 + (tc + 1) * 128],
                                        rhs=wd_sb[sl][:, db * 512:(db + 1) * 512],
                                        start=(ic == 0), stop=(ic == ICN - 1)))
                        wd_mm_last[sl] = mmn
                        down_last = mmn
                    for tc in range(4):
                        u = tc % 2
                        x1_ld[u] = em.dma(
                            "sync", f"sx1r{u}", [("sv", x1t_war[u])],
                            x1t[u][:, :1024],
                            x1s[(tb * 4 + tc) * 128:(tb * 4 + tc + 1) * 128,
                                dhh * 1024:(dhh + 1) * 1024])
                        ev = 0
                        for db in range(2):
                            ev = em.op(
                                "vector", "sv",
                                [("sp", down_last),
                                 ("sost", ostc[u])],
                                lambda e, db=db, tc=tc, u=u, dhh=dhh:
                                e.tensor_tensor(
                                    out=ost[u][:, db * 512:(db + 1) * 512],
                                    in0=ps8[tc * 2 + db][:],
                                    in1=dscb_sb[:, dhh * 1024 + db * 512:
                                                dhh * 1024 + (db + 1) * 512],
                                    op=ALU.mult))
                            bank_war[tc * 2 + db] = ("sv", ev)
                        rs = em.op("vector", "sv",
                                   [("sv", ev),
                                    (f"sx1r{u}", x1_ld[u])],
                                   lambda e, u=u: e.tensor_tensor(
                                       out=ost[u][:, :1024],
                                       in0=ost[u][:, :1024],
                                       in1=x1t[u][:, :1024], op=ALU.add))
                        x1t_war[u] = rs
                        ostc[u] = em.dma(
                            "sync", "sost", [("sv", rs)],
                            out[(tb * 4 + tc) * 128:(tb * 4 + tc + 1) * 128,
                                dhh * 1024:(dhh + 1) * 1024],
                            ost[u][:, :1024])

        em.wait("sync", "sost", em.C["sost"])
        for s_, v_ in ([("sqst0", QKV_ST[0]), ("sqst1", QKV_ST[1]),
                        ("sx1st0", X1_ST[0]), ("sx1st1", X1_ST[1]),
                        ("sxnst0", XN_ST[0]), ("sxnst1", XN_ST[1])] +
                       [(f"sfs{i}", CAST_ST[i]) for i in range(6)]):
            em.wait("sync", s_, v_)

        with nc.Block() as block:
            em.replay(block)

    return nc


def _prep_inputs(inputs):
    x = np.asarray(inputs["L0_attn_resid_pre"], dtype=np.float32)
    tw = lambda a: np.ascontiguousarray(np.asarray(a).T)
    base = {
        "wqT": tw(inputs["L0_attn_Wq"]).astype(np.float32, copy=False),
        "wkT": tw(inputs["L0_attn_Wk"]).astype(np.float32, copy=False),
        "wvT": tw(inputs["L0_attn_Wv"]).astype(np.float32, copy=False),
        "woT": tw(inputs["L0_attn_Wo"]).astype(np.float32, copy=False),
        "wgT": tw(inputs["L0_ffn_gate_w"]),
        "wuT": tw(inputs["L0_ffn_up_w"]),
        "wdT": tw(inputs["L0_ffn_down_w"]),
        "gsc": np.ascontiguousarray(
            np.asarray(inputs["L0_ffn_gate_scale"], dtype=np.float32)
            .reshape(ICN, 128).T),
        "usc": np.ascontiguousarray(
            np.asarray(inputs["L0_ffn_up_scale"], dtype=np.float32)
            .reshape(ICN, 128).T),
        "dsc": np.ascontiguousarray(
            np.asarray(inputs["L0_ffn_down_scale"], dtype=np.float32)
            .reshape(DC, 128).T),
        "idbf": np.eye(128, dtype=ml_dtypes.bfloat16),
        "idf32": np.eye(128, dtype=np.float32),
        "cvec": np.broadcast_to(
            np.array([EPS, ISQ, 0.0, 0.0, 0.0, 0.0, 0.0, 0.0],
                     dtype=np.float32)[None, :], (128, 8)).copy(),
    }
    j = np.arange(S, dtype=np.float32)
    base["jrowi"] = np.broadcast_to(j[None, :], (128, S)).copy()
    slopes = np.array(_alibi_slopes(H), dtype=np.float32)
    base["biasbi"] = np.broadcast_to(
        (slopes[:, None] * j[None, :])[:, None, :], (H, 128, S)).copy()
    base["dscb"] = np.broadcast_to(
        np.asarray(inputs["L0_ffn_down_scale"], dtype=np.float32)[None, :],
        (128, D)).copy()
    jl = np.arange(128, dtype=np.float32)
    base["cmaski"] = np.where(jl[None, :] > jl[:, None], np.float32(NEG),
                              np.float32(0.0)).astype(np.float32)
    return x, base


def kernel(**inputs):
    x, base = _prep_inputs(inputs)
    nc = build_program()
    in_maps = []
    for c in range(NC):
        m = dict(base)
        m["x"] = np.ascontiguousarray(
            x[c * BL:(c + 1) * BL].reshape(T, D))
        in_maps.append(m)
    res = run_bass_kernel_spmd(nc, in_maps, core_ids=list(range(NC)))
    outs = [np.asarray(r["out"], dtype=np.float32).reshape(BL, S, D)
            for r in res.results]
    return np.concatenate(outs, axis=0)


# version 32
# speedup vs baseline: 1.0671x; 1.0416x over previous
import math
from collections import defaultdict
from contextlib import ExitStack

import numpy as np
import ml_dtypes
import concourse.bass as bass
import concourse.mybir as mybir
from concourse.bass_utils import run_bass_kernel_spmd

F32 = mybir.dt.float32
BF16 = mybir.dt.bfloat16
I32 = mybir.dt.int32
AX = mybir.AxisListType.X
ALU = mybir.AluOpType
AF = mybir.ActivationFunctionType

B, S, D, H, DH = 16, 2048, 2048, 16, 128
NC = 8
BL = B // NC
T = BL * S
I = 8192
DC = D // 128
ICN = I // 128
TT = T // 128
TB = T // 512
EPS = 1e-5
ISQ = 1.0 / math.sqrt(DH)
NEG = -1.0e30


def _alibi_slopes(n):
    def pow2(m):
        start = 2 ** (-(2 ** (-(math.log2(m) - 3))))
        return [start * start ** i for i in range(m)]
    if math.log2(n).is_integer():
        return pow2(n)
    c = 2 ** math.floor(math.log2(n))
    return pow2(c) + _alibi_slopes(2 * c)[0::2][: n - c]


class Em:
    def __init__(self):
        self.streams = defaultdict(list)
        self.C = defaultdict(int)
        self.waited = defaultdict(dict)
        self.sems = {}

    def wait(self, eng, sem, val):
        if val <= 0:
            return
        if self.waited[eng].get(sem, -1) >= val:
            return
        self.waited[eng][sem] = val
        self.streams[eng].append(
            lambda e, s=sem, v=val: e.wait_ge(self.sems[s], v))

    def op(self, eng, sem, waits, f, n=1):
        for s, v in waits:
            self.wait(eng, s, v)
        self.C[sem] += n
        cnt = self.C[sem]
        self.streams[eng].append(
            lambda e, f=f, s=sem, n=n: f(e).then_inc(self.sems[s], n))
        return cnt

    def dma(self, eng, sem, waits, out_ap, in_ap):
        for s, v in waits:
            self.wait(eng, s, v)
        self.C[sem] += 16
        cnt = self.C[sem]
        self.streams[eng].append(
            lambda e, o=out_ap, i=in_ap, s=sem: e.dma_start(out=o, in_=i)
            .then_inc(self.sems[s], 16))
        return cnt

    def raw(self, eng, f):
        self.streams[eng].append(f)

    def replay(self, block):
        order = [("sync", block.sync), ("tensor", block.tensor),
                 ("vector", block.vector), ("scalar", block.scalar),
                 ("gpsimd", block.gpsimd)]
        for name, starter in order:
            if not self.streams[name]:
                continue
            fns = self.streams[name]

            def body(e, fns=fns):
                for fn in fns:
                    fn(e)
            starter(body)


def build_program(dbg=(), upto=3):
    nc = bass.Bass("TRN2", target_bir_lowering=False)

    def dram(name, shape, dt, kind):
        k = "ExternalOutput" if name in dbg and kind == "Internal" else kind
        return nc.dram_tensor(name, shape, dt, kind=k).ap()

    x = dram("x", [T, D], F32, "ExternalInput")
    wqT = dram("wqT", [D, D], F32, "ExternalInput")
    wkT = dram("wkT", [D, D], F32, "ExternalInput")
    wvT = dram("wvT", [D, D], F32, "ExternalInput")
    woT = dram("woT", [D, D], F32, "ExternalInput")
    wgT = dram("wgT", [D, I], I32, "ExternalInput")
    wuT = dram("wuT", [D, I], I32, "ExternalInput")
    wdT = dram("wdT", [I, D], I32, "ExternalInput")
    gsc = dram("gsc", [128, ICN], F32, "ExternalInput")
    usc = dram("usc", [128, ICN], F32, "ExternalInput")
    dsc = dram("dsc", [128, DC], F32, "ExternalInput")
    idbf = dram("idbf", [128, 128], BF16, "ExternalInput")
    idf32 = dram("idf32", [128, 128], F32, "ExternalInput")
    jrowi = dram("jrowi", [128, S], F32, "ExternalInput")
    biasbi = dram("biasbi", [H, 128, S], F32, "ExternalInput")
    dscb = dram("dscb", [128, D], F32, "ExternalInput")
    cvec = dram("cvec", [128, 8], F32, "ExternalInput")
    cmaski = dram("cmaski", [128, 128], F32, "ExternalInput")

    out = dram("out", [T, D], F32, "ExternalOutput")

    qs = dram("qs", [H, DH, T], BF16, "Internal")
    ks = dram("ks", [H, DH, T], BF16, "Internal")
    vs = dram("vs", [H, DH, T], BF16, "Internal")
    x1s = dram("x1s", [T, D], F32, "Internal")
    xn2t = dram("xn2t", [TT, 128, D], BF16, "Internal")
    wgbf = dram("wgbf", [ICN, 128, DC, 128], BF16, "Internal")
    wubf = dram("wubf", [ICN, 128, DC, 128], BF16, "Internal")
    wdbf = dram("wdbf", [ICN, 128, D], BF16, "Internal")
    wobf = dram("wobf", [D, D], BF16, "Internal")

    em = Em()

    with ExitStack() as ctx:
        def sb(name, shape, dt):
            return ctx.enter_context(nc.sbuf_tensor(name, shape, dt))

        def psum(name, shape, dt):
            return ctx.enter_context(nc.psum_tensor(name, shape, dt))

        def sem(name):
            em.sems[name] = ctx.enter_context(nc.semaphore(name))
            return name

        for s_ in ["sv", "sa", "sp", "sg",
                   "c_ld",
                   "sx0", "sx1",
                   "swt0", "swt1", "swt2",
                   "sqst0", "sqst1",
                   "sfw0", "sfw1", "sfw2", "sfw3", "sfw4", "sfw5",
                   "sfs0", "sfs1", "sfs2", "sfs3", "sfs4", "sfs5",
                   "skv0", "skv1",
                   "sbs0", "sbs1",
                   "swo0", "swo1",
                   "sxr0", "sxr1",
                   "sx1st0", "sx1st1",
                   "sx1r0", "sx1r1",
                   "sxnst0", "sxnst1",
                   "sxn0", "sxn1",
                   "sgu0", "sgu1", "sgu2",
                   "swd0", "swd1", "swd2", "swd3",
                   "sost",
                   ]:
            sem(s_)

        id_bf = sb("id_bf", [128, 128], BF16)
        cstg = [sb(f"cstg{i}", [128, 512], I32) for i in range(6)]
        cbf = [sb(f"cbf{i}", [128, 512], BF16) for i in range(6)]
        id_f32 = sb("id_f32", [128, 128], F32)
        gsc_sb = sb("gsc_sb", [128, ICN], F32)
        usc_sb = sb("usc_sb", [128, ICN], F32)
        dsc_sb = sb("dsc_sb", [128, DC], F32)
        cv = sb("cv", [128, 8], F32)

        for dst, src in [(id_bf, idbf), (id_f32, idf32), (gsc_sb, gsc),
                         (usc_sb, usc), (dsc_sb, dsc), (cv, cvec)]:
            em.dma("sync", "c_ld", [], dst[:], src)
        CLD = em.C["c_ld"]

        cast_jobs = []
        for rb in range(DC):
            for cc in range(4):
                cast_jobs.append(("f32", woT, wobf, rb, cc))
        WO_TILES = len(cast_jobs)
        for csrc, cdst, nrb, ncc, kind in (
                (wgT, wgbf, DC, I // 512, "gu"),
                (wuT, wubf, DC, I // 512, "gu"),
                (wdT, wdbf, ICN, D // 512, "wd")):
            for rb in range(nrb):
                for cc in range(ncc):
                    cast_jobs.append((kind, csrc, cdst, rb, cc))
        cast_state = {"ji": 0, "ld": 0, "done": [0] * 6,
                      "ldc": [0] * 6, "stc": [0] * 6, "wo": [0] * 6}

        def emit_casts(k):
            st_ = cast_state

            def issue_load():
                ji = st_["ld"]
                if ji >= len(cast_jobs):
                    return
                kind, csrc, cdst, rb, cc = cast_jobs[ji]
                sl = ji % 6
                stg = (cstg[sl][:] if kind in ("gu", "wd")
                       else cstg[sl][:].bitcast(F32))
                em.dma("gpsimd", f"sfw{sl}", [("sa", st_["done"][sl])],
                       stg, csrc[rb * 128:(rb + 1) * 128,
                                 cc * 512:(cc + 1) * 512])
                st_["ldc"][sl] = em.C[f"sfw{sl}"]
                st_["ld"] = ji + 1

            while st_["ld"] < min(st_["ji"] + 6, len(cast_jobs)):
                issue_load()
            for _ in range(k):
                ji = st_["ji"]
                if ji >= len(cast_jobs):
                    return
                kind, csrc, cdst, rb, cc = cast_jobs[ji]
                sl = ji % 6
                if kind in ("gu", "wd"):
                    st_["done"][sl] = em.op(
                        "scalar", "sa",
                        [(f"sfw{sl}", st_["ldc"][sl]),
                         (f"sfs{sl}", st_["stc"][sl])],
                        lambda e, sl=sl: e.activation(
                            out=cbf[sl][:], in_=cstg[sl][:],
                            func=AF.Copy, bias=-128.0, scale=1.0))
                else:
                    st_["done"][sl] = em.op(
                        "scalar", "sa",
                        [(f"sfw{sl}", st_["ldc"][sl]),
                         (f"sfs{sl}", st_["stc"][sl])],
                        lambda e, sl=sl: e.activation(
                            out=cbf[sl][:], in_=cstg[sl][:].bitcast(F32),
                            func=AF.Copy))
                if kind == "gu":
                    dstap = (cdst[4 * cc:4 * cc + 4, :, rb, :]
                             .rearrange("j p i -> p j i"))
                    srcap = cbf[sl][:].rearrange("p (j i) -> p j i", i=128)
                elif kind == "wd":
                    dstap = cdst[rb][:, cc * 512:(cc + 1) * 512]
                    srcap = cbf[sl][:]
                else:
                    dstap = cdst[rb * 128:(rb + 1) * 128,
                                 cc * 512:(cc + 1) * 512]
                    srcap = cbf[sl][:]
                st_["stc"][sl] = em.dma(
                    "gpsimd", f"sfs{sl}", [("sa", st_["done"][sl])],
                    dstap, srcap)
                if ji == WO_TILES - 1:
                    st_["wo"] = [em.C[f"sfs{i}"] for i in range(6)]
                st_["ji"] = ji + 1
                if st_["ld"] < len(cast_jobs):
                    issue_load()


        with ExitStack() as p1:
            def sb1(name, shape, dt):
                return p1.enter_context(nc.sbuf_tensor(name, shape, dt))
            xts = [sb1(f"xts{i}", [128, D], F32) for i in range(2)]
            xnbf = [sb1(f"xnbf{i}", [128, D], BF16) for i in range(2)]
            stat = sb1("stat", [128, 16], F32)
            xnfm = sb1("xnfm", [128, DC * T], BF16)
            wstage = [sb1(f"wstage{i}", [128, D], F32) for i in range(2)]
            wbf = [sb1(f"wbf{i}", [128, D], BF16) for i in range(2)]
            qstage = [sb1(f"qstage{i}", [128, 512], BF16) for i in range(2)]
            ln_ps = [p1.enter_context(nc.psum_tensor(f"lnps{i}", [128, 1024], BF16))
                     for i in range(2)]
            qkv_ps = [p1.enter_context(nc.psum_tensor(f"qkvps{i}", [128, 512], F32))
                      for i in range(6)]

            def st(slot, j):
                return stat[:, slot * 8 + j: slot * 8 + j + 1]

            norm_cnt = [0, 0]
            evac_cnt = [0, 0]
            ln_tile_done = [0] * TT
            ld_n = [0, 0]
            ln_tp = [0, 0]
            st1_saved = {}
            st2_saved = {}
            sq_saved = {}

            def ln_load(tt):
                sl = tt % 2
                ld_n[sl] = em.dma("sync", f"sx{sl}", [("sv", norm_cnt[sl])],
                                  xts[sl][:], x[tt * 128:(tt + 1) * 128, :])

            def ln_stage1(tt):
                sl = tt % 2
                c1 = em.op("vector", "sv", [(f"sx{sl}", ld_n[sl])],
                           lambda e, sl=sl: e.tensor_reduce(
                               out=st(sl, 0), in_=xts[sl][:], axis=AX,
                               op=ALU.add))
                em.op("vector", "sv", [("sv", c1)],
                      lambda e, sl=sl: e.tensor_scalar(
                          out=st(sl, 0), in0=st(sl, 0), scalar1=1.0 / D,
                          scalar2=None, op0=ALU.mult))
                sq = em.op("scalar", "sa", [(f"sx{sl}", ld_n[sl]),
                                            ("sp", ln_tp[sl])],
                           lambda e, sl=sl: e.activation(
                               out=xnbf[sl][:], in_=xts[sl][:],
                               func=AF.Square, accum_out=st(sl, 1)))
                sq_saved[tt] = sq
                c3 = em.op("vector", "sv", [("sv", c1 + 1)],
                           lambda e, sl=sl: e.tensor_scalar(
                               out=st(sl, 2), in0=st(sl, 0),
                               scalar1=st(sl, 0), scalar2=None, op0=ALU.mult))
                st1_saved[tt] = em.op(
                    "vector", "sv", [("sv", c3), ("sa", sq)],
                    lambda e, sl=sl: e.scalar_tensor_tensor(
                        out=st(sl, 3), in0=st(sl, 1), scalar=1.0 / D,
                        in1=st(sl, 2), op0=ALU.mult, op1=ALU.subtract))

            def ln_stage2(tt):
                sl = tt % 2
                em.wait("scalar", "c_ld", CLD)
                l1 = em.op("scalar", "sa", [("sv", st1_saved[tt])],
                           lambda e, sl=sl: e.activation(
                               out=st(sl, 4), in_=st(sl, 3), func=AF.Ln,
                               bias=cv[:, 0:1], scale=1.0))
                st2_saved[tt] = em.op(
                    "scalar", "sa", [("sa", l1)],
                    lambda e, sl=sl: e.activation(
                        out=st(sl, 5), in_=st(sl, 4), func=AF.Exp,
                        bias=0.0, scale=-0.5))

            def ln_stage3(tt):
                sl = tt % 2
                norm_cnt[sl] = em.op(
                    "vector", "sv", [("sa", st2_saved[tt])],
                    lambda e, sl=sl: e.tensor_scalar(
                        out=xnbf[sl][:], in0=xts[sl][:], scalar1=st(sl, 0),
                        scalar2=st(sl, 5), op0=ALU.subtract, op1=ALU.mult))
                em.wait("tensor", "c_ld", CLD)
                em.wait("tensor", "sv", norm_cnt[sl])
                for half in range(2):
                    em.wait("tensor", "sv", evac_cnt[half])
                    tp = 0
                    for k in range(8):
                        dc = half * 8 + k
                        tp = em.op("tensor", "sp", [],
                                   lambda e, sl=sl, dc=dc, k=k, half=half:
                                   e.transpose(
                                       out=ln_ps[half][:, k * 128:(k + 1) * 128],
                                       in_=xnbf[sl][:, dc * 128:(dc + 1) * 128],
                                       identity=id_bf[:]))
                    evac_cnt[half] = em.op(
                        "vector", "sv", [("sp", tp)],
                        lambda e, half=half, tt=tt: e.tensor_copy(
                            out=xnfm[:].rearrange("p (dc t) -> p dc t", dc=DC)
                            [:, half * 8:half * 8 + 8, tt * 128:(tt + 1) * 128],
                            in_=ln_ps[half][:].rearrange(
                                "p (dc t) -> p dc t", t=128)))
                    ln_tp[sl] = tp
                ln_tile_done[tt] = evac_cnt[1]

            wT_srcs = [(wqT, qs), (wkT, ks), (wvT, vs)]
            cast_cnt = [0, 0]
            mm_last_for_wbf = [0, 0]
            mm_last_for_wstage = [0, 0]
            evq_cnt = [0] * 6
            evq_eng = [None] * 6
            qst_cnt = [0, 0]
            n_ev = 0
            NW = 3 * DC
            wload_cnt = [0, 0]

            def w_load(gi):
                pi_, oc_ = gi // DC, gi % DC
                sl = gi % 2
                wload_cnt[sl] = em.dma(
                    "sync", f"swt{sl}", [("sv", cast_cnt[sl])],
                    wstage[sl][:].rearrange("p (dc o) -> p dc o", o=128),
                    wT_srcs[pi_][0][:, oc_ * 128:(oc_ + 1) * 128]
                    .rearrange("(dc p) o -> p dc o", p=128))

            def w_cast(gi):
                sl = gi % 2
                cast_cnt[sl] = em.op(
                    "vector", "sv",
                    [(f"swt{sl}", wload_cnt[sl]), ("sp", mm_last_for_wbf[sl])],
                    lambda e, sl=sl: e.tensor_copy(
                        out=wbf[sl][:], in_=wstage[sl][:]))

            ln_load(0)
            w_load(0)
            w_cast(0)
            w_load(1)
            for tt in range(TT + 1):
                emit_casts(3)
                if tt >= 1:
                    ln_stage3(tt - 1)
                if tt < TT:
                    if tt + 1 < TT:
                        ln_load(tt + 1)
                    ln_stage1(tt)
                    ln_stage2(tt)
            LN_DONE = em.C["sv"]

            for gi in range(NW):
                pi, oc = gi // DC, gi % DC
                dst = wT_srcs[pi][1]
                sl = gi % 2
                if gi + 1 < NW:
                    w_cast(gi + 1)
                if gi + 2 < NW:
                    w_load(gi + 2)
                em.wait("tensor", "sv", cast_cnt[sl])
                for tbg, tbs in enumerate(((0,), (1, 2, 3), (4, 5), (6, 7))):
                    bset = (tbg % 2) * 3
                    for j in range(len(tbs)):
                        bk = bset + j
                        if evq_eng[bk] is not None:
                            em.wait("tensor", evq_eng[bk], evq_cnt[bk])
                    em.wait("tensor", "sv",
                            ln_tile_done[tbs[-1] * 4 + 3])
                    mmn = 0
                    for dc in range(DC):
                        for j, tb in enumerate(tbs):
                            bk = bset + j
                            mmn = em.op(
                                "tensor", "sp", [],
                                lambda e, sl=sl, dc=dc, bk=bk, tb=tb:
                                e.matmul(
                                    out=qkv_ps[bk][:],
                                    lhsT=wbf[sl][:, dc * 128:(dc + 1) * 128],
                                    rhs=xnfm[:, dc * T + tb * 512:
                                             dc * T + tb * 512 + 512],
                                    start=(dc == 0), stop=(dc == DC - 1)))
                    mm_last_for_wbf[sl] = mmn
                    for j, tb in enumerate(tbs):
                        bk = bset + j
                        ssl = n_ev % 2
                        ssem = f"sqst{ssl}"
                        if pi == 0:
                            ev = em.op("vector", "sv",
                                       [("sp", mmn), (ssem, qst_cnt[ssl])],
                                       lambda e, bk=bk, ssl=ssl:
                                       e.tensor_scalar(
                                           out=qstage[ssl][:],
                                           in0=qkv_ps[bk][:],
                                           scalar1=ISQ, scalar2=None,
                                           op0=ALU.mult))
                        else:
                            ev = em.op("vector", "sv",
                                       [("sp", mmn), (ssem, qst_cnt[ssl])],
                                       lambda e, bk=bk, ssl=ssl:
                                       e.tensor_copy(
                                           out=qstage[ssl][:],
                                           in_=qkv_ps[bk][:]))
                        evq_cnt[bk], evq_eng[bk] = ev, "sv"
                        qst_cnt[ssl] = em.dma(
                            "sync", ssem, [("sv", ev)],
                            dst[oc][:, tb * 512: tb * 512 + 512],
                            qstage[ssl][:])
                        n_ev += 1
                    emit_casts(6)
            QKV_ST = (em.C["sqst0"], em.C["sqst1"])

        def barrier(engines=("sync", "tensor", "vector", "scalar"),
                    dma_sems=()):
            targets = [(s_, em.C[s_]) for s_ in ("sv", "sa", "sp")]
            targets += [(s_, em.C[s_]) for s_ in dma_sems]
            for eng in engines:
                for s_, v_ in targets:
                    em.wait(eng, s_, v_)

        ALLE = ("sync", "tensor", "vector", "scalar", "gpsimd")
        barrier(engines=ALLE, dma_sems=("sqst0", "sqst1"))
        em.wait("gpsimd", "sg", em.C["sg"])

        if upto < 2:
            em.wait("sync", "sqst0", QKV_ST[0])
            em.wait("sync", "sqst1", QKV_ST[1])
            with nc.Block() as block:
                em.replay(block)
            return nc

        with ExitStack() as p2:
            def sb2(name, shape, dt):
                return p2.enter_context(nc.sbuf_tensor(name, shape, dt))
            kh = [sb2(f"kh{i}", [128, S], BF16) for i in range(2)]
            qh = [sb2(f"qh{i}", [128, S], BF16) for i in range(2)]
            vh = [sb2(f"vh{i}", [128, S], BF16) for i in range(2)]
            vt = [sb2(f"vt{i}", [128, S], BF16) for i in range(2)]
            sf = [sb2(f"sf{i}", [128, S], F32) for i in range(3)]
            pb = [sb2(f"pb{i}", [128, S], BF16) for i in range(3)]
            pt = [sb2(f"pt{i}", [128, S], BF16) for i in range(3)]
            biash = [sb2(f"biash{i}", [128, S], F32) for i in range(2)]
            cmask = sb2("cmask", [128, 128], F32)
            Zb = sb2("Zb", [128, H * S], BF16)
            st2 = sb2("st2", [128, 32], F32)
            wo_sb = [sb2(f"wo_sb{i}", [128, DC * 256], BF16) for i in range(2)]
            xres = [sb2(f"xres{i}", [128, 256], F32) for i in range(2)]
            x1st = [sb2(f"x1st{i}", [128, 256], F32) for i in range(2)]
            sc_ps = [p2.enter_context(nc.psum_tensor(f"scps{i}", [128, 512], F32))
                     for i in range(4)]
            tp_ps = [p2.enter_context(nc.psum_tensor(f"tpps{i}", [128, 1024], BF16))
                     for i in range(2)]
            z_ps = [p2.enter_context(nc.psum_tensor(f"zps{i}", [128, 128], F32))
                    for i in range(2)]
            o_ps = sc_ps

            em.dma("sync", "c_ld", [], cmask[:], cmaski)
            CLD2 = em.C["c_ld"]

            slot_war = [0, 0]
            tp_war = [("sa", 0), ("sa", 0)]
            p1_war = [0, 0, 0, 0]
            kv_ld = [0, 0]
            slopes = _alibi_slopes(H)
            ccg = 0

            def stv(j):
                return st2[:, j:j + 1]

            def emit_head_loads(s, h):
                sl = (s * H + h) % 2
                sem_ = f"skv{sl}"
                for buf, src_ in ((kh, ks), (qh, qs), (vh, vs)):
                    em.dma("sync", sem_, [("sp", em.C["sp"])],
                           buf[sl][:], src_[h][:, s * S:(s + 1) * S])
                kv_ld[sl] = em.C[sem_]

            for s in range(BL):
                if s == 0:
                    emit_head_loads(s, 0)
                    emit_head_loads(s, 1)
                NCH = H * 16
                p1l = {}
                negl = {}
                expl = {}
                recl = {}
                pscl = {}
                egrp = {}
                flast = {}
                gmm = {}
                hl = {}
                vtr = {}
                bias_war = [0, 0]

                def A_(i):
                    nonlocal ccg
                    h, qt = i // 16, i % 16
                    sl = (s * H + h) % 2
                    qp = i % 3
                    kend = (qt + 1) * 128
                    qsl = qh[sl][:, qt * 128:(qt + 1) * 128]
                    for c in range((kend + 511) // 512):
                        N = min(512, kend - c * 512)
                        pp = ccg % 4
                        mmn = em.op(
                            "tensor", "sp",
                            [("sa", p1_war[pp]), (f"skv{sl}", kv_ld[sl])],
                            lambda e, pp=pp, c=c, N=N, sl=sl, qsl=qsl:
                            e.matmul(out=sc_ps[pp][:, :N], lhsT=qsl,
                                     rhs=kh[sl][:, c * 512: c * 512 + N],
                                     start=True, stop=True))
                        p1_war[pp] = em.op(
                            "scalar", "sa",
                            [("sp", mmn), ("sa", expl.get(i - 3, 0))],
                            lambda e, pp=pp, c=c, N=N, qp=qp: e.activation(
                                out=sf[qp][:, c * 512: c * 512 + N],
                                in_=sc_ps[pp][:, :N], func=AF.Copy))
                        p1l[i] = p1_war[pp]
                        ccg += 1

                def TT_(i):
                    h, qt = i // 16, i % 16
                    qp = i % 3
                    kend = (qt + 1) * 128
                    em.wait("vector", "c_ld", CLD2)
                    c1 = em.op("vector", "sv", [("sa", p1l[i])],
                               lambda e, qp=qp, kend=kend: e.tensor_scalar(
                                   out=sf[qp][:, :kend], in0=sf[qp][:, :kend],
                                   scalar1=1024.0, scalar2=-1024.0,
                                   op0=ALU.min, op1=ALU.max))
                    c2 = em.op("vector", "sv", [("sv", c1)],
                               lambda e, qp=qp, qt=qt: e.tensor_tensor(
                                   out=sf[qp][:, qt * 128:(qt + 1) * 128],
                                   in0=sf[qp][:, qt * 128:(qt + 1) * 128],
                                   in1=cmask[:], op=ALU.add))
                    c3 = em.op("vector", "sv",
                               [("sv", c2), (f"sbs{h % 2}", bias_war[h % 2])],
                               lambda e, qp=qp, kend=kend, h=h: e.tensor_tensor(
                                   out=sf[qp][:, :kend], in0=sf[qp][:, :kend],
                                   in1=biash[h % 2][:, :kend], op=ALU.add))
                    negl[i] = em.op(
                        "vector", "sv", [("sv", c3)],
                        lambda e, qp=qp, kend=kend: e.tensor_reduce(
                            out=stv(qp * 5 + 3), in_=sf[qp][:, :kend],
                            axis=AX, op=ALU.max, negate=True))

                def C_(i):
                    h, qt = i // 16, i % 16
                    qp = i % 3
                    kend = (qt + 1) * 128
                    expl[i] = em.op(
                        "scalar", "sa",
                        [("sv", negl[i]),
                         ("sp", egrp.get(i - 3, [(0, 0, 0, 0)])[-1][3]),
                         ("sv", recl.get(i - 3, 0))],
                        lambda e, qp=qp, kend=kend: e.activation(
                            out=pb[qp][:, :kend], in_=sf[qp][:, :kend],
                            func=AF.Exp, bias=stv(qp * 5 + 3), scale=1.0,
                            accum_out=stv(qp * 5 + 1)))

                def D_(i):
                    qp = i % 3
                    recl[i] = em.op("vector", "sv", [("sa", expl[i])],
                                    lambda e, qp=qp: e.reciprocal(
                                        out=stv(qp * 5 + 2),
                                        in_=stv(qp * 5 + 1)))

                def PS_(i):
                    qp = i % 3
                    kend = (i % 16 + 1) * 128
                    pscl[i] = em.op(
                        "vector", "sv", [("sv", recl[i])],
                        lambda e, qp=qp, kend=kend: e.tensor_scalar(
                            out=pb[qp][:, :kend], in0=pb[qp][:, :kend],
                            scalar1=stv(qp * 5 + 2), scalar2=None,
                            op0=ALU.mult))

                def E_(i):
                    qp = i % 3
                    kch = i % 16 + 1
                    groups = []
                    em.wait("tensor", "sv", pscl[i])
                    for g in range((kch + 7) // 8):
                        gb = g % 2
                        ntl = min(8, kch - g * 8)
                        em.wait("tensor", tp_war[gb][0], tp_war[gb][1])
                        tpn = 0
                        for k in range(ntl):
                            kc = g * 8 + k
                            tpn = em.op(
                                "tensor", "sp", [],
                                lambda e, qp=qp, kc=kc, gb=gb, k=k: e.transpose(
                                    out=tp_ps[gb][:, k * 128:(k + 1) * 128],
                                    in_=pb[qp][:, kc * 128:(kc + 1) * 128],
                                    identity=id_bf[:]))
                        groups.append((g, gb, ntl, tpn))
                    egrp[i] = groups

                def F_(i):
                    qp = i % 3
                    last = 0
                    for g, gb, ntl, tpn in egrp[i]:
                        last = em.op(
                            "scalar", "sa",
                            [("sp", tpn), ("sp", gmm.get(i - 3, 0))],
                            lambda e, qp=qp, g=g, gb=gb, ntl=ntl: e.activation(
                                out=pt[qp][:, g * 1024: g * 1024 + ntl * 128],
                                in_=tp_ps[gb][:, :ntl * 128], func=AF.Copy))
                        tp_war[gb] = ("sa", last)
                    flast[i] = last

                def G_(i):
                    h, qt = i // 16, i % 16
                    sl = (s * H + h) % 2
                    qp = i % 3
                    zp = i % 2
                    kch = qt + 1
                    em.wait("tensor", "sa", flast[i])
                    em.wait("tensor", "sv", vtr[h])
                    em.wait("tensor", "sv", hl.get(i - 2, 0))
                    mmn = 0
                    for kc in range(kch):
                        mmn = em.op(
                            "tensor", "sp", [],
                            lambda e, qp=qp, kc=kc, sl=sl, zp=zp, kch=kch:
                            e.matmul(
                                out=z_ps[zp][:],
                                lhsT=vt[sl][:, kc * 128:(kc + 1) * 128],
                                rhs=pt[qp][:, kc * 128:(kc + 1) * 128],
                                start=(kc == 0), stop=(kc == kch - 1)))
                    gmm[i] = mmn

                def H_(i):
                    h, qt = i // 16, i % 16
                    zp = i % 2
                    hl[i] = em.op(
                        "vector", "sv", [("sp", gmm[i])],
                        lambda e, zp=zp, h=h, qt=qt: e.tensor_copy(
                            out=Zb[:, h * S + qt * 128: h * S + (qt + 1) * 128],
                            in_=z_ps[zp][:]))

                def headstart(h):
                    sl = (s * H + h) % 2
                    emit_casts(1)
                    if h == 0:
                        bias_war[0] = em.dma(
                            "sync", "sbs0", [("sv", em.C["sv"])],
                            biash[0][:], biasbi[0])
                    em.wait("tensor", f"skv{sl}", kv_ld[sl])
                    vt_ready = 0
                    for g in range(2):
                        em.wait("tensor", tp_war[g][0], tp_war[g][1])
                        tpn = 0
                        for k in range(8):
                            kc = g * 8 + k
                            tpn = em.op(
                                "tensor", "sp", [],
                                lambda e, sl=sl, kc=kc, g=g, k=k: e.transpose(
                                    out=tp_ps[g][:, k * 128:(k + 1) * 128],
                                    in_=vh[sl][:, kc * 128:(kc + 1) * 128],
                                    identity=id_bf[:]))
                        vt_ready = em.op(
                            "vector", "sv",
                            [("sp", tpn), ("sp", slot_war[sl])],
                            lambda e, sl=sl, g=g: e.tensor_copy(
                                out=vt[sl][:, g * 1024:(g + 1) * 1024],
                                in_=tp_ps[g][:]))
                        tp_war[g] = ("sv", vt_ready)
                    vtr[h] = vt_ready
                    nh = s * H + h + 1
                    if nh < BL * H and nh >= 2:
                        emit_head_loads(nh // H, nh % H)

                for i in range(NCH + 5):
                    if i < NCH:
                        h, qt = i // 16, i % 16
                        if qt == 0:
                            headstart(h)
                        if qt == 1 and h + 1 < H:
                            bias_war[(h + 1) % 2] = em.dma(
                                "sync", f"sbs{(h + 1) % 2}",
                                [("sv", em.C["sv"])],
                                biash[(h + 1) % 2][:], biasbi[h + 1])
                        A_(i)
                    if 0 <= i - 1 < NCH:
                        TT_(i - 1)
                    if 0 <= i - 2 < NCH:
                        C_(i - 2)
                    if 0 <= i - 3 < NCH:
                        D_(i - 3)
                        PS_(i - 3)
                    if 0 <= i - 4 < NCH:
                        E_(i - 4)
                        F_(i - 4)
                    if 0 <= i - 5 < NCH:
                        G_(i - 5)
                        H_(i - 5)
                    if (i - 5) >= 0 and (i - 5) % 16 == 15:
                        hdone = (i - 5) // 16
                        slot_war[(s * H + hdone) % 2] = em.C["sp"]
                Z_DONE = em.C["sv"]

                barrier()
                emit_casts(150)
                wo_ld = [0, 0]
                ops_war = [0, 0]
                x1_evac = [0, 0]
                bank_evac = [0, 0, 0, 0]
                xr_ld = [0, 0]
                x1_stc = [0, 0]
                for ci in range(6):
                    em.wait("sync", f"sfs{ci}", cast_state["wo"][ci])

                def wo_load(dblk):
                    wsl = dblk % 2
                    wo_ld[wsl] = em.dma(
                        "sync", f"swo{wsl}", [("sp", ops_war[wsl])],
                        wo_sb[wsl][:].rearrange("p (oc d) -> p oc d", d=256),
                        wobf[:, dblk * 256:(dblk + 1) * 256]
                        .rearrange("(oc p) d -> p oc d", p=128))

                wo_load(0)
                for dblk in range(8):
                    wsl = dblk % 2
                    if dblk + 1 < 8:
                        wo_load(dblk + 1)
                    for tc in range(16):
                        pp4 = (dblk * 16 + tc) % 4
                        pp2 = (dblk * 16 + tc) % 2
                        xr_ld[pp2] = em.dma(
                            "sync", f"sxr{pp2}", [("sv", x1_evac[pp2])],
                            xres[pp2][:],
                            x[s * S + tc * 128: s * S + (tc + 1) * 128,
                              dblk * 256:(dblk + 1) * 256])
                        em.wait("tensor", f"swo{wsl}", wo_ld[wsl])
                        em.wait("tensor", "sv", bank_evac[pp4])
                        mmn = 0
                        for oc in range(DC):
                            mmn = em.op(
                                "tensor", "sp", [],
                                lambda e, pp4=pp4, oc=oc, tc=tc, wsl=wsl:
                                e.matmul(
                                    out=o_ps[pp4][:, :256],
                                    lhsT=Zb[:, oc * S + tc * 128:
                                            oc * S + (tc + 1) * 128],
                                    rhs=wo_sb[wsl][:, oc * 256:(oc + 1) * 256],
                                    start=(oc == 0), stop=(oc == DC - 1)))
                        ops_war[wsl] = mmn
                        ev_ = em.op(
                            "vector", "sv",
                            [("sp", mmn), (f"sxr{pp2}", xr_ld[pp2]),
                             (f"sx1st{pp2}", x1_stc[pp2])],
                            lambda e, pp4=pp4, pp2=pp2: e.tensor_tensor(
                                out=x1st[pp2][:], in0=o_ps[pp4][:, :256],
                                in1=xres[pp2][:], op=ALU.add))
                        x1_evac[pp2] = ev_
                        bank_evac[pp4] = ev_
                        x1_stc[pp2] = em.dma(
                            "sync", f"sx1st{pp2}", [("sv", ev_)],
                            x1s[s * S + tc * 128: s * S + (tc + 1) * 128,
                                dblk * 256:(dblk + 1) * 256],
                            x1st[pp2][:])

                barrier(dma_sems=("sx1st0", "sx1st1"))
                xn_stc = [0, 0]
                ln2_nrm = [0, 0]
                l2_tp = [0, 0]
                l2_ld = [0, 0]
                l2_s1 = {}
                l2_s2 = {}

                def l2_load(t16):
                    sl = t16 % 2
                    l2_ld[sl] = em.dma(
                        "sync", f"sx1r{sl}", [("sv", ln2_nrm[sl])],
                        sf[sl][:], x1s[(s * 16 + t16) * 128:
                                       (s * 16 + t16 + 1) * 128, :])

                def l2_stage1(t16):
                    sl = t16 % 2
                    c1 = em.op("vector", "sv", [(f"sx1r{sl}", l2_ld[sl])],
                               lambda e, sl=sl: e.tensor_reduce(
                                   out=stv(16 + sl * 6), in_=sf[sl][:],
                                   axis=AX, op=ALU.add))
                    em.op("vector", "sv", [("sv", c1)],
                          lambda e, sl=sl: e.tensor_scalar(
                              out=stv(16 + sl * 6), in0=stv(16 + sl * 6),
                              scalar1=1.0 / D, scalar2=None, op0=ALU.mult))
                    sq = em.op("scalar", "sa",
                               [(f"sx1r{sl}", l2_ld[sl]),
                                ("sp", l2_tp[sl])],
                               lambda e, sl=sl: e.activation(
                                   out=pb[sl][:], in_=sf[sl][:],
                                   func=AF.Square,
                                   accum_out=stv(17 + sl * 6)))
                    c3 = em.op("vector", "sv", [("sv", c1 + 1)],
                               lambda e, sl=sl: e.tensor_scalar(
                                   out=stv(18 + sl * 6), in0=stv(16 + sl * 6),
                                   scalar1=stv(16 + sl * 6), scalar2=None,
                                   op0=ALU.mult))
                    l2_s1[t16] = em.op(
                        "vector", "sv", [("sv", c3), ("sa", sq)],
                        lambda e, sl=sl: e.scalar_tensor_tensor(
                            out=stv(19 + sl * 6), in0=stv(17 + sl * 6),
                            scalar=1.0 / D, in1=stv(18 + sl * 6),
                            op0=ALU.mult, op1=ALU.subtract))

                def l2_stage2(t16):
                    sl = t16 % 2
                    l1 = em.op("scalar", "sa", [("sv", l2_s1[t16])],
                               lambda e, sl=sl: e.activation(
                                   out=stv(20 + sl * 6), in_=stv(19 + sl * 6),
                                   func=AF.Ln, bias=cv[:, 0:1], scale=1.0))
                    l2_s2[t16] = em.op(
                        "scalar", "sa", [("sa", l1)],
                        lambda e, sl=sl: e.activation(
                            out=stv(21 + sl * 6), in_=stv(20 + sl * 6),
                            func=AF.Exp, bias=0.0, scale=-0.5))

                def l2_stage3(t16):
                    sl = t16 % 2
                    tt = s * 16 + t16
                    nrm = em.op("vector", "sv", [("sa", l2_s2[t16])],
                                lambda e, sl=sl: e.tensor_scalar(
                                    out=pb[sl][:], in0=sf[sl][:],
                                    scalar1=stv(16 + sl * 6),
                                    scalar2=stv(21 + sl * 6),
                                    op0=ALU.subtract, op1=ALU.mult))
                    ln2_nrm[sl] = nrm
                    em.wait("tensor", "sv", nrm)
                    evl = 0
                    for g in range(2):
                        em.wait("tensor", tp_war[g][0], tp_war[g][1])
                        tpn = 0
                        for k in range(8):
                            dc = g * 8 + k
                            tpn = em.op("tensor", "sp", [],
                                        lambda e, sl=sl, dc=dc, g=g, k=k:
                                        e.transpose(
                                            out=tp_ps[g][:, k * 128:(k + 1) * 128],
                                            in_=pb[sl][:, dc * 128:(dc + 1) * 128],
                                            identity=id_bf[:]))
                        evl = em.op("scalar", "sa",
                                    [("sp", tpn),
                                     (f"sxnst{sl}", xn_stc[sl])],
                                    lambda e, sl=sl, g=g: e.activation(
                                        out=pt[sl][:, g * 1024:(g + 1) * 1024],
                                        in_=tp_ps[g][:], func=AF.Copy))
                        tp_war[g] = ("sa", evl)
                        l2_tp[sl] = tpn
                    xn_stc[sl] = em.dma("sync", f"sxnst{sl}", [("sa", evl)],
                                        xn2t[tt], pt[sl][:])

                l2_load(0)
                for t16 in range(17):
                    if t16 >= 1:
                        l2_stage3(t16 - 1)
                    if t16 < 16:
                        if t16 + 1 < 16:
                            l2_load(t16 + 1)
                        l2_stage1(t16)
                        l2_stage2(t16)
                emit_casts(150)
                barrier(dma_sems=("sxnst0", "sxnst1"))

            emit_casts(len(cast_jobs))
            CAST_ST = tuple(em.C[f"sfs{i}"] for i in range(6))
            X1_ST = (em.C["sx1st0"], em.C["sx1st1"])
            XN_ST = (em.C["sxnst0"], em.C["sxnst1"])

        if upto < 3:
            for s_, v_ in ([("sx1st0", X1_ST[0]), ("sx1st1", X1_ST[1]),
                            ("sxnst0", XN_ST[0]), ("sxnst1", XN_ST[1]),
                            ("sqst0", QKV_ST[0]), ("sqst1", QKV_ST[1])] +
                           [(f"sfs{i}", CAST_ST[i]) for i in range(6)]):
                em.wait("sync", s_, v_)
            with nc.Block() as block:
                em.replay(block)
            return nc

        barrier(engines=ALLE,
                dma_sems=("sxnst0", "sxnst1", "sx1st0", "sx1st1", "sfs0",
                          "sfs1", "sfs2", "sfs3", "sfs4", "sfs5",
                          "sqst0", "sqst1"))
        em.wait("gpsimd", "sg", em.C["sg"])
        with ExitStack() as p3:
            def sb3(name, shape, dt):
                return p3.enter_context(nc.sbuf_tensor(name, shape, dt))
            xn2_sb = [sb3(f"xn2sb{i}", [128, DC * 512], BF16) for i in range(2)]
            wg_sb = [sb3(f"wgsb{i}", [128, DC * 128], BF16) for i in range(3)]
            wu_sb = [sb3(f"wusb{i}", [128, DC * 128], BF16) for i in range(3)]
            h_sb = sb3("hsb", [128, ICN * 512], BF16)
            wd_sb = [sb3(f"wdsb{i}", [128, 1024], BF16) for i in range(4)]
            dscb_sb = sb3("dscb_sb", [128, D], F32)
            x1t = [sb3(f"x1t{i}", [128, D], F32) for i in range(2)]
            ost = [sb3(f"ost{i}", [128, D], F32) for i in range(2)]
            ps8 = [p3.enter_context(nc.psum_tensor(f"ps8_{i}", [128, 512], F32))
                   for i in range(8)]
            em.dma("sync", "c_ld", [], dscb_sb[:], dscb)
            CLD3 = em.C["c_ld"]
            gu_ps = ps8

            xn_ld = [0, 0]
            gu_ld = [0, 0, 0]
            wd_ld = [0, 0, 0, 0]
            x1_ld = [0, 0]
            x1t_war = [0, 0]
            gu_mm_last = [0, 0, 0]
            xn_mm_last = [0, 0]
            wd_mm_last = [0, 0, 0, 0]
            bank_war = [("sv", 0)] * 8
            down_last = 0
            ostc = [0, 0]
            for tb in range(TB):
                xsl = tb % 2
                xn_ld[xsl] = em.dma(
                    "sync", f"sxn{xsl}", [("sp", xn_mm_last[xsl])],
                    xn2_sb[xsl][:].rearrange("p (dc tt c) -> p dc tt c",
                                             dc=DC, tt=4),
                    xn2t[tb * 4:(tb + 1) * 4]
                    .rearrange("tt p (dc c) -> p dc tt c", c=128))
                for ic in range(ICN):
                    sl = ic % 3
                    sgu = f"sgu{sl}"
                    em.dma("gpsimd", sgu, [("sp", gu_mm_last[sl])],
                           wg_sb[sl][:],
                           wgbf[ic].rearrange("p dc i -> p (dc i)"))
                    gu_ld[sl] = em.dma(
                        "gpsimd", sgu, [],
                        wu_sb[sl][:],
                        wubf[ic].rearrange("p dc i -> p (dc i)"))
                    em.wait("tensor", sgu, gu_ld[sl])
                    em.wait("tensor", f"sxn{xsl}", xn_ld[xsl])
                    em.wait("tensor", bank_war[2 * sl][0], bank_war[2 * sl][1])
                    em.wait("tensor", bank_war[2 * sl + 1][0],
                            bank_war[2 * sl + 1][1])
                    g_mm = u_mm = 0
                    for dc in range(DC):
                        g_mm = em.op(
                            "tensor", "sp", [],
                            lambda e, sl=sl, dc=dc, xsl=xsl: e.matmul(
                                out=gu_ps[2 * sl][:],
                                lhsT=wg_sb[sl][:, dc * 128:(dc + 1) * 128],
                                rhs=xn2_sb[xsl][:, dc * 512:(dc + 1) * 512],
                                start=(dc == 0), stop=(dc == DC - 1)))
                    for dc in range(DC):
                        u_mm = em.op(
                            "tensor", "sp", [],
                            lambda e, sl=sl, dc=dc, xsl=xsl: e.matmul(
                                out=gu_ps[2 * sl + 1][:],
                                lhsT=wu_sb[sl][:, dc * 128:(dc + 1) * 128],
                                rhs=xn2_sb[xsl][:, dc * 512:(dc + 1) * 512],
                                start=(dc == 0), stop=(dc == DC - 1)))
                    gu_mm_last[sl] = u_mm
                    xn_mm_last[xsl] = u_mm
                    sv_ = em.op("scalar", "sa",
                                [("sp", g_mm), ("sp", down_last)],
                                lambda e, sl=sl, ic=ic: e.activation(
                                    out=h_sb[:, ic * 512:(ic + 1) * 512],
                                    in_=gu_ps[2 * sl][:], func=AF.Silu,
                                    scale=gsc_sb[:, ic:ic + 1]))
                    bank_war[2 * sl] = ("sa", sv_)
                    hm = em.op(
                        "vector", "sv", [("sp", u_mm), ("sa", sv_)],
                        lambda e, sl=sl, ic=ic: e.scalar_tensor_tensor(
                            out=h_sb[:, ic * 512:(ic + 1) * 512],
                            in0=gu_ps[2 * sl + 1][:],
                            scalar=usc_sb[:, ic:ic + 1],
                            in1=h_sb[:, ic * 512:(ic + 1) * 512],
                            op0=ALU.mult, op1=ALU.mult))
                    bank_war[2 * sl + 1] = ("sv", hm)
                H_DONE = em.C["sv"]
                em.wait("vector", "c_ld", CLD3)
                for dhh in range(2):
                    for ic in range(ICN):
                        sl = ic % 4
                        swd = f"swd{sl}"
                        wd_ld[sl] = em.dma(
                            "scalar", swd, [("sp", wd_mm_last[sl])],
                            wd_sb[sl][:],
                            wdbf[ic][:, dhh * 1024:(dhh + 1) * 1024])
                        em.wait("tensor", swd, wd_ld[sl])
                        if ic == 0:
                            em.wait("tensor", "sv", H_DONE)
                        mmn = 0
                        for tc in range(4):
                            for db in range(2):
                                if ic == 0:
                                    em.wait("tensor", bank_war[tc * 2 + db][0],
                                            bank_war[tc * 2 + db][1])
                                mmn = em.op(
                                    "tensor", "sp", [],
                                    lambda e, sl=sl, ic=ic, tc=tc,
                                    db=db: e.matmul(
                                        out=ps8[tc * 2 + db][:],
                                        lhsT=h_sb[:, ic * 512 + tc * 128:
                                                  ic * 512 + (tc + 1) * 128],
                                        rhs=wd_sb[sl][:, db * 512:(db + 1) * 512],
                                        start=(ic == 0), stop=(ic == ICN - 1)))
                        wd_mm_last[sl] = mmn
                        down_last = mmn
                    for tc in range(4):
                        u = tc % 2
                        x1_ld[u] = em.dma(
                            "sync", f"sx1r{u}", [("sv", x1t_war[u])],
                            x1t[u][:, :1024],
                            x1s[(tb * 4 + tc) * 128:(tb * 4 + tc + 1) * 128,
                                dhh * 1024:(dhh + 1) * 1024])
                        ev = 0
                        for db in range(2):
                            ev = em.op(
                                "vector", "sv",
                                [("sp", down_last),
                                 ("sost", ostc[u])],
                                lambda e, db=db, tc=tc, u=u, dhh=dhh:
                                e.tensor_tensor(
                                    out=ost[u][:, db * 512:(db + 1) * 512],
                                    in0=ps8[tc * 2 + db][:],
                                    in1=dscb_sb[:, dhh * 1024 + db * 512:
                                                dhh * 1024 + (db + 1) * 512],
                                    op=ALU.mult))
                            bank_war[tc * 2 + db] = ("sv", ev)
                        rs = em.op("vector", "sv",
                                   [("sv", ev),
                                    (f"sx1r{u}", x1_ld[u])],
                                   lambda e, u=u: e.tensor_tensor(
                                       out=ost[u][:, :1024],
                                       in0=ost[u][:, :1024],
                                       in1=x1t[u][:, :1024], op=ALU.add))
                        x1t_war[u] = rs
                        ostc[u] = em.dma(
                            "sync", "sost", [("sv", rs)],
                            out[(tb * 4 + tc) * 128:(tb * 4 + tc + 1) * 128,
                                dhh * 1024:(dhh + 1) * 1024],
                            ost[u][:, :1024])

        em.wait("sync", "sost", em.C["sost"])
        for s_, v_ in ([("sqst0", QKV_ST[0]), ("sqst1", QKV_ST[1]),
                        ("sx1st0", X1_ST[0]), ("sx1st1", X1_ST[1]),
                        ("sxnst0", XN_ST[0]), ("sxnst1", XN_ST[1])] +
                       [(f"sfs{i}", CAST_ST[i]) for i in range(6)]):
            em.wait("sync", s_, v_)

        with nc.Block() as block:
            em.replay(block)

    return nc


def _prep_inputs(inputs):
    x = np.asarray(inputs["L0_attn_resid_pre"], dtype=np.float32)
    tw = lambda a: np.ascontiguousarray(np.asarray(a).T)
    base = {
        "wqT": tw(inputs["L0_attn_Wq"]).astype(np.float32, copy=False),
        "wkT": tw(inputs["L0_attn_Wk"]).astype(np.float32, copy=False),
        "wvT": tw(inputs["L0_attn_Wv"]).astype(np.float32, copy=False),
        "woT": tw(inputs["L0_attn_Wo"]).astype(np.float32, copy=False),
        "wgT": tw(inputs["L0_ffn_gate_w"]),
        "wuT": tw(inputs["L0_ffn_up_w"]),
        "wdT": tw(inputs["L0_ffn_down_w"]),
        "gsc": np.ascontiguousarray(
            np.asarray(inputs["L0_ffn_gate_scale"], dtype=np.float32)
            .reshape(ICN, 128).T),
        "usc": np.ascontiguousarray(
            np.asarray(inputs["L0_ffn_up_scale"], dtype=np.float32)
            .reshape(ICN, 128).T),
        "dsc": np.ascontiguousarray(
            np.asarray(inputs["L0_ffn_down_scale"], dtype=np.float32)
            .reshape(DC, 128).T),
        "idbf": np.eye(128, dtype=ml_dtypes.bfloat16),
        "idf32": np.eye(128, dtype=np.float32),
        "cvec": np.broadcast_to(
            np.array([EPS, ISQ, 0.0, 0.0, 0.0, 0.0, 0.0, 0.0],
                     dtype=np.float32)[None, :], (128, 8)).copy(),
    }
    j = np.arange(S, dtype=np.float32)
    base["jrowi"] = np.broadcast_to(j[None, :], (128, S)).copy()
    slopes = np.array(_alibi_slopes(H), dtype=np.float32)
    base["biasbi"] = np.broadcast_to(
        (slopes[:, None] * j[None, :])[:, None, :], (H, 128, S)).copy()
    base["dscb"] = np.broadcast_to(
        np.asarray(inputs["L0_ffn_down_scale"], dtype=np.float32)[None, :],
        (128, D)).copy()
    jl = np.arange(128, dtype=np.float32)
    base["cmaski"] = np.where(jl[None, :] > jl[:, None], np.float32(NEG),
                              np.float32(0.0)).astype(np.float32)
    return x, base


def kernel(**inputs):
    x, base = _prep_inputs(inputs)
    nc = build_program()
    in_maps = []
    for c in range(NC):
        m = dict(base)
        m["x"] = np.ascontiguousarray(
            x[c * BL:(c + 1) * BL].reshape(T, D))
        in_maps.append(m)
    res = run_bass_kernel_spmd(nc, in_maps, core_ids=list(range(NC)))
    outs = [np.asarray(r["out"], dtype=np.float32).reshape(BL, S, D)
            for r in res.results]
    return np.concatenate(outs, axis=0)
